# version 06
# baseline (speedup 1.0000x reference)
import numpy as np
from contextlib import ExitStack

import concourse.bass as bass
import concourse.mybir as mybir
from concourse.bass_utils import run_bass_kernel_spmd
from concourse import library_config
from concourse.library_overlay import lower_extended_insts

F32 = mybir.dt.float32
BF16 = mybir.dt.bfloat16
I16 = mybir.dt.int16
AF = mybir.ActivationFunctionType
ALU = mybir.AluOpType
AX = mybir.AxisListType

D = 1024
H = 16
HD = 64
NPT = 8
NEG = -1.0e30
EPS = 1e-5
INV = 0.125

OFFSETS = sorted(set(range(33)) | {48, 64, 96, 128, 192, 256, 384, 512, 768, 1024, 1536})
NEAR = [o for o in OFFSETS if o <= 192]
FAR = [o for o in OFFSETS if o > 192]
FAR_M = [o // 128 for o in FAR]
NRANK = len(NEAR)
NOFF = len(OFFSETS)
RING = 13
KFR = 3
WIN = 320
PWIN = 384
PW = 256
ENGINES = ("tensor", "vector", "scalar", "gpsimd", "sync")


def _host_consts():
    c = {}
    c["identity"] = np.eye(128, dtype=np.float32)
    idx_ext = np.full((128, 2 * WIN), -1, np.int16)
    for n in range(128):
        for r, o in enumerate(NEAR):
            col = n + 192 - o
            if 0 <= col < WIN:
                idx_ext[n, 2 * col] = 2 * r
                idx_ext[n, 2 * col + 1] = 2 * r + 1
    c["idx_ext"] = idx_ext
    idx_bld = np.full((128, 2 * NRANK), -1, np.int16)
    for n in range(128):
        for r, o in enumerate(NEAR):
            w = n - o + 256
            idx_bld[n, 2 * r] = 2 * w
            idx_bld[n, 2 * r + 1] = 2 * w + 1
    c["idx_bld"] = idx_bld
    for j in (0, 1):
        m = np.zeros((128, NRANK), np.float32)
        for n in range(128):
            for r, o in enumerate(NEAR):
                if 128 * j + n < o:
                    m[n, r] = NEG
        c[f"mask{j}"] = m
    return c


class Buf:
    __slots__ = ("last_w", "readers", "name")

    def __init__(self, name=""):
        self.last_w = None
        self.readers = []
        self.name = name


class Prog:
    def __init__(self, serial=False, same_engine_barriers=True):
        self.ops = {e: [] for e in ENGINES}
        self.serial = serial
        self.seb = same_engine_barriers
        self.last_tok = None

    def emit(self, eng, fn, reads=(), writes=(), tag="", small=False, dma=False):
        is_dma = dma or eng == "sync"
        deps = []
        if self.serial and self.last_tok is not None:
            deps.append(self.last_tok)
        for b in reads:
            if b.last_w is not None:
                deps.append(b.last_w)
        for b in writes:
            deps.extend(b.readers)
            if b.last_w is not None:
                deps.append(b.last_w)
        uniq = {}
        for (pe, pi, pd) in deps:
            if pe == eng and not pd and not self.serial:
                if eng in ("tensor", "gpsimd"):
                    continue
                if (not self.ops[pe][pi].get("small") and not self.seb):
                    continue
            uniq[(pe, pi)] = (pe, pi, pd)
        idx = len(self.ops[eng])
        self.ops[eng].append({"fn": fn, "deps": list(uniq.values()),
                              "inc": is_dma, "tag": tag, "small": small,
                              "step": 16 if is_dma else 1})
        for (pe, pi, _d) in uniq.values():
            self.ops[pe][pi]["inc"] = True
        tok = (eng, idx, is_dma)
        self.last_tok = tok
        for b in reads:
            b.readers.append(tok)
        for b in writes:
            b.last_w = tok
            b.readers = []
        return tok

    def make_runner(self, sems):
        prefix = {}
        for e in ENGINES:
            cnt = 0
            pr = []
            for op in self.ops[e]:
                if op["inc"]:
                    cnt += op["step"]
                pr.append(cnt)
            prefix[e] = pr

        def run_engine(eng_obj, e):
            waited = {}
            for op in self.ops[e]:
                for (pe, pi, _d) in op["deps"]:
                    val = prefix[pe][pi]
                    if waited.get(pe, 0) < val:
                        eng_obj.wait_ge(sems[pe], val)
                        waited[pe] = val
                last = op["fn"](eng_obj)
                if op["inc"]:
                    assert last is not None, f"op returned None: {op['tag']}"
                    last.then_inc(sems[e], op["step"])
        return run_engine


def build_nc(n_tiles=32, dumps=(), serial=False, seb=True):
    nc = bass.Bass("TRN2", target_bir_lowering=False)
    P = Prog(serial=serial, same_engine_barriers=seb)
    dump_regs = {}

    x_in = nc.dram_tensor("x", [4096, D], F32, kind="ExternalInput").ap()
    y_out = nc.dram_tensor("y", [4096, D], F32, kind="ExternalOutput").ap()
    qkv_wT = nc.dram_tensor("qkv_wT", [D, 3 * D], F32, kind="ExternalInput").ap()
    gate_wT = nc.dram_tensor("gate_wT", [D, D], F32, kind="ExternalInput").ap()
    out_wT = nc.dram_tensor("out_wT", [D, D], F32, kind="ExternalInput").ap()
    WrT_d = nc.dram_tensor("WrT", [D, D], F32, kind="ExternalInput").ap()
    WkT_d = nc.dram_tensor("WkT", [D, D], F32, kind="ExternalInput").ap()
    WvT_d = nc.dram_tensor("WvT", [D, D], F32, kind="ExternalInput").ap()
    qkv_b = nc.dram_tensor("qkv_b", [1, 3 * D], F32, kind="ExternalInput").ap()
    gate_b = nc.dram_tensor("gate_b", [1, D], F32, kind="ExternalInput").ap()
    out_b = nc.dram_tensor("out_b", [1, D], F32, kind="ExternalInput").ap()
    sedup_in = nc.dram_tensor("sedup", [128, NOFF], F32, kind="ExternalInput").ap()
    pb_in = nc.dram_tensor("pbconst", [128, H * NOFF], F32, kind="ExternalInput").ap()
    gain_in = nc.dram_tensor("gain16", [128, H], F32, kind="ExternalInput").ap()
    mu_in = nc.dram_tensor("mu_in", [128, 2 * NPT], F32, kind="ExternalInput").ap()
    ident_in = nc.dram_tensor("identity", [128, 128], F32, kind="ExternalInput").ap()
    idxe_in = nc.dram_tensor("idx_ext", [128, 2 * WIN], I16, kind="ExternalInput").ap()
    idxb_in = nc.dram_tensor("idx_bld", [128, 2 * NRANK], I16, kind="ExternalInput").ap()
    mask0_in = nc.dram_tensor("mask0", [128, NRANK], F32, kind="ExternalInput").ap()

    WSRC = {
        "q": qkv_wT[:, 0:D], "k2": qkv_wT[:, D:2 * D], "v2": qkv_wT[:, 2 * D:3 * D],
        "g": gate_wT, "o": out_wT, "r": WrT_d, "k": WkT_d, "v": WvT_d,
    }
    whl = {n: nc.dram_tensor(f"whl_{n}", [4, 128, NPT * 2 * PW], BF16,
                             kind="Internal").ap() for n in WSRC}
    bias_hl = nc.dram_tensor("bias_hl", [2, 5 * D], BF16, kind="Internal").ap()
    BSEG = {"q": 0, "k2": 1, "v2": 2, "g": 3, "o": 4}
    BSRC = {"q": qkv_b[:, 0:D], "k2": qkv_b[:, D:2 * D],
            "v2": qkv_b[:, 2 * D:3 * D], "g": gate_b, "o": out_b}

    with ExitStack() as stack:
        e = stack.enter_context

        def sb(name, shape, dt=F32):
            return e(nc.sbuf_tensor(name, shape, dt))

        k_tm = sb("k_tm", [128, RING * D])
        v_tm = sb("v_tm", [128, RING * D])
        k_fm = sb("k_fm", [128, KFR * D])
        CT = 2
        x_ch = sb("x_ch", [128, CT * D])
        xlnT_ch = sb("xlnT_ch", [128, CT * D])
        a_ch = sb("a_ch", [128, CT * D])
        b_ch = sb("b_ch", [128, CT * D])
        wbuf = [sb(f"wbuf{i}", [128, NPT * 2 * PW], BF16) for i in range(2)]
        st_hi = sb("st_hi", [128, CT * D], BF16)
        st_lo = sb("st_lo", [128, CT * D], BF16)
        q_fm = sb("q_fm", [128, D])
        band_scr = sb("band_scr", [128, 2 * WIN])
        qslab = sb("qslab", [128, 3 * H * NOFF])
        qs_cmp = qslab[:, 0:H * NOFF]
        qk_cmp = qslab[:, H * NOFF:2 * H * NOFF]
        p_cmp = qslab[:, 2 * H * NOFF:3 * H * NOFF]

        prod = sb("prod", [128, D])
        facc = sb("facc", [128, D])
        pband = [sb(f"pband{i}", [128, PWIN]) for i in range(2)]
        pbandT = sb("pbandT", [128, PWIN])
        stat = sb("stat", [128, 8])
        red16 = sb("red16", [128, 2 * H])
        rec16 = sb("rec16", [128, H])
        carryA = sb("carryA", [128, NPT])
        carryC = sb("carryC", [128, NPT])
        sedup = sb("sedup_sb", [128, NOFF])
        pbc = sb("pbc_sb", [128, H * NOFF])
        gain16 = sb("gain16_sb", [128, H])
        mu = sb("mu_sb", [128, 2 * NPT])
        omu = sb("omu_sb", [128, 2 * NPT])
        ident = sb("ident_sb", [128, 128])
        ones2 = sb("ones2_sb", [2, 128], BF16)
        bias_stage = sb("bias_stage", [2, 2 * PW], BF16)
        idxe = sb("idxe_sb", [128, 2 * WIN], I16)
        idxb = sb("idxb_sb", [128, 2 * NRANK], I16)
        mask0 = sb("mask0_sb", [128, NRANK])

        psum = [e(nc.psum_tensor(f"ps{i}", [128, 512], F32)) for i in range(8)]
        sems = {en: e(nc.semaphore(f"sem_{en}")) for en in ENGINES}

        B_ktm = [Buf(f"ktm{s}") for s in range(RING)]
        B_vtm = [Buf(f"vtm{s}") for s in range(RING)]
        B_kfm = [Buf(f"kfm{s}") for s in range(KFR)]
        B_x = [Buf(f"x{i}") for i in range(2)]
        B_xlnT = [Buf(f"xlnT{i}") for i in range(2)]
        B_a = [Buf(f"a{i}") for i in range(2)]
        B_b = [Buf(f"b{i}") for i in range(2)]
        B_w = [[Buf(f"w{i}_{hh}") for hh in range(2)] for i in range(2)]
        B_st = [Buf(f"st{i}") for i in range(2)]
        B_qfm = Buf("qfm")
        B_band = [Buf(f"band{i}") for i in range(2)]
        B_qs = [Buf(f"qs{g}") for g in range(2)]
        B_qk = [Buf(f"qk{g}") for g in range(2)]
        B_qkf = Buf("qkf")
        B_p = [Buf(f"p{g}") for g in range(2)]
        B_prod = Buf("prod")
        B_facc = [Buf(f"facc{g}") for g in range(2)]
        B_pband = [Buf(f"pband{i}") for i in range(2)]
        B_pbandT = Buf("pbandT")
        B_stat = Buf("stat")
        B_red = [Buf(f"red{g}") for g in range(2)]
        B_rec = [Buf(f"rec{g}") for g in range(2)]
        B_carryA = Buf("carryA")
        B_carryC = Buf("carryC")
        B_ps = [Buf(f"ps{i}") for i in range(8)]
        B_const = Buf("const")
        B_bias = [Buf("bias0"), Buf("bias1")]
        B_wsc = Buf("wscratch")
        B_out = [Buf(f"out{t}") for t in range(n_tiles)]

        def xh(i):
            return x_ch[:, i * D:(i + 1) * D]

        def xlh(i):
            return xlnT_ch[:, i * D:(i + 1) * D]

        def ah(i):
            return a_ch[:, i * D:(i + 1) * D]

        def bh(i):
            return b_ch[:, i * D:(i + 1) * D]

        def dump(tag, src_ap, cols, reads):
            if tag not in dumps or tag in dump_regs:
                return
            t_ = nc.dram_tensor("dbg_" + tag, [128, cols], F32,
                                kind="ExternalOutput").ap()
            dump_regs[tag] = True
            P.emit("sync", lambda s, t_=t_, src_ap=src_ap: s.dma_start(
                out=t_, in_=src_ap), reads=list(reads), tag="dump_" + tag)

        for dst, src in ((sedup[:, :], sedup_in), (pbc[:, :], pb_in),
                         (gain16[:, :], gain_in), (mu[:, :], mu_in),
                         (ident[:, :], ident_in),
                         (idxe[:, :], idxe_in),
                         (idxb[:, :], idxb_in), (mask0[:, :], mask0_in)):
            P.emit("sync", lambda s, d=dst, so=src: s.dma_start(out=d, in_=so),
                   writes=[B_const], tag="cload")

        def _loadlib(g):
            g.load_library(library_config.local_scatter)
            return g.nop()
        P.emit("gpsimd", _loadlib, tag="loadlib")
        P.emit("vector", lambda v: nc.vector.tensor_scalar(
            omu[:, :], mu[:, :], -1.0, 1.0, ALU.mult, ALU.add),
            reads=[B_const], writes=[B_const], tag="omu", small=True)
        for t_, bufs in ((k_tm, B_ktm), (v_tm, B_vtm), (k_fm, B_kfm)):
            P.emit("vector", lambda v, tt=t_: nc.vector.memset(tt[:, :], 0.0),
                   writes=list(bufs), tag="zring")
        P.emit("vector", lambda v: nc.vector.memset(carryA[:, :], 0.0),
               writes=[B_carryA], tag="zcA", small=True)
        P.emit("vector", lambda v: nc.vector.memset(stat[:, 6:7], float(EPS)),
               writes=[B_stat], tag="eps", small=True)
        P.emit("vector", lambda v: nc.vector.memset(ones2[:, :], 1.0),
               writes=[B_const], tag="ones2", small=True)
        P.emit("vector", lambda v: nc.vector.memset(carryC[:, :], 0.0),
               writes=[B_carryC], tag="zcC", small=True)

        for name in WSRC:
            src3 = WSRC[name].rearrange("(a p) c -> p a c", p=128)
            for pi_ in range(4):
                c0 = pi_ * PW
                xv = x_ch[:, :].rearrange("p (a c) -> p a c", a=NPT)
                P.emit("sync", lambda s, src3=src3, c0=c0, xv=xv: s.dma_start(
                    out=xv, in_=src3[:, :, c0:c0 + PW]),
                    writes=[B_x[0], B_x[1]], tag="wsp_ld")
                P.emit("scalar", lambda s: nc.scalar.copy(
                    st_hi[:, :], x_ch[:, :]),
                    reads=[B_x[0], B_x[1]], writes=[B_st[0], B_st[1]],
                    tag="wsp_hi")
                P.emit("vector", lambda v: nc.vector.tensor_tensor(
                    st_lo[:, :], x_ch[:, :], st_hi[:, :], ALU.subtract),
                    reads=[B_x[0], B_x[1], B_st[0], B_st[1]],
                    writes=[B_st[0], B_st[1]], tag="wsp_lo")
                w5 = whl[name][pi_, :, :].rearrange(
                    "p (a h c) -> p a h c", a=NPT, h=2)
                P.emit("sync", lambda s, w5=w5: s.dma_start(
                    out=w5[:, :, 0, :],
                    in_=st_hi[:, :].rearrange("p (a c) -> p a c", a=NPT)),
                    reads=[B_st[0], B_st[1]], writes=[B_wsc], tag="wsp_sh")
                P.emit("gpsimd", lambda g, w5=w5: g.dma_start(
                    out=w5[:, :, 1, :],
                    in_=st_lo[:, :].rearrange("p (a c) -> p a c", a=NPT)),
                    reads=[B_st[0], B_st[1]], writes=[B_wsc], tag="wsp_sl",
                    dma=True)

        for bname, bsrc in BSRC.items():
            seg = BSEG[bname]
            P.emit("sync", lambda s, bsrc=bsrc: s.dma_start(
                out=prod[0:1, :], in_=bsrc),
                writes=[B_prod], tag="bsp_ld")
            P.emit("scalar", lambda s: nc.scalar.copy(
                st_hi[0:1, 0:D], prod[0:1, :]),
                reads=[B_prod], writes=[B_st[0]], tag="bsp_hi", small=True)
            P.emit("vector", lambda v: nc.vector.tensor_tensor(
                st_lo[0:1, 0:D], prod[0:1, :], st_hi[0:1, 0:D], ALU.subtract),
                reads=[B_prod, B_st[0]], writes=[B_st[0]], tag="bsp_lo",
                small=True)
            P.emit("sync", lambda s, seg=seg: s.dma_start(
                out=bias_hl[0:1, seg * D:(seg + 1) * D], in_=st_hi[0:1, 0:D]),
                reads=[B_st[0]], writes=[B_wsc], tag="bsp_sh")
            P.emit("sync", lambda s, seg=seg: s.dma_start(
                out=bias_hl[1:2, seg * D:(seg + 1) * D], in_=st_lo[0:1, 0:D]),
                reads=[B_st[0]], writes=[B_wsc], tag="bsp_sl")

        def ln_tile(x_ap, bufs_x, out_ap, bufs_out):
            s_sum = stat[:, 0:1]
            s_sq = stat[:, 1:2]
            s_mean = stat[:, 2:3]
            s_var = stat[:, 3:4]
            s_rstd = stat[:, 4:5]
            s_nmr = stat[:, 5:6]
            s_eps = stat[:, 6:7]
            P.emit("vector", lambda v: nc.vector.reduce_sum(s_sum, x_ap, AX.X),
                   reads=list(bufs_x), writes=[B_stat], tag="ln_sum", small=True)
            P.emit("scalar", lambda s: nc.scalar.activation(
                prod[:, :], x_ap, AF.Square, accum_out=s_sq),
                reads=list(bufs_x), writes=[B_prod, B_stat], tag="ln_sq", small=True)
            P.emit("vector", lambda v: nc.vector.tensor_scalar(
                s_mean, s_sum, 1.0 / D, None, ALU.mult),
                reads=[B_stat], writes=[B_stat], tag="ln_v1", small=True)
            P.emit("vector", lambda v: nc.vector.tensor_tensor(
                s_var, s_mean, s_mean, ALU.mult),
                reads=[B_stat], writes=[B_stat], tag="ln_v2", small=True)
            P.emit("vector", lambda v: nc.vector.scalar_tensor_tensor(
                s_var, s_sq, 1.0 / D, s_var, ALU.mult, ALU.subtract),
                reads=[B_stat], writes=[B_stat], tag="ln_v3", small=True)
            P.emit("scalar", lambda s: nc.scalar.activation(
                s_var, s_var, AF.Sqrt, bias=s_eps),
                reads=[B_stat], writes=[B_stat], tag="ln_sqrt", small=True)
            P.emit("vector", lambda v: nc.vector.reciprocal(s_rstd, s_var),
                   reads=[B_stat], writes=[B_stat], tag="ln_r1", small=True)
            P.emit("vector", lambda v: nc.vector.scalar_tensor_tensor(
                s_nmr, s_mean, -1.0, s_rstd, ALU.mult, ALU.mult),
                reads=[B_stat], writes=[B_stat], tag="ln_r2", small=True)
            P.emit("vector", lambda v: nc.vector.tensor_scalar(
                out_ap, x_ap, s_rstd, s_nmr, ALU.mult, ALU.add),
                reads=list(bufs_x) + [B_stat], writes=list(bufs_out), tag="ln_n")

        def tm2fm(src_ap, bufs_src, dst_ap, bufs_dst, tag):
            def tmm(tn):
                last = None
                for p2 in range(NPT):
                    bank = 6 + (p2 // 4)
                    last = nc.tensor.transpose(
                        psum[bank][:, (p2 % 4) * 128:(p2 % 4) * 128 + 128],
                        src_ap[:, p2 * 128:(p2 + 1) * 128], ident[:, :])
                return last
            P.emit("tensor", tmm, reads=list(bufs_src) + [B_const],
                   writes=[B_ps[6], B_ps[7]], tag=tag + "mm")
            P.emit("scalar", lambda s: nc.scalar.copy(dst_ap[:, 0:512], psum[6][:, :]),
                   reads=[B_ps[6]], writes=list(bufs_dst), tag=tag + "e0")
            P.emit("vector", lambda v: nc.vector.tensor_copy(
                dst_ap[:, 512:1024], psum[7][:, :]),
                reads=[B_ps[7]], writes=list(bufs_dst), tag=tag + "e1")

        def split_half(src_ap, bufs_src, half, tag):
            hi_ap = st_hi[:, half * D:(half + 1) * D]
            lo_ap = st_lo[:, half * D:(half + 1) * D]
            P.emit("scalar", lambda s, hi_ap=hi_ap: nc.scalar.copy(
                hi_ap, src_ap),
                reads=list(bufs_src), writes=[B_st[half]], tag=tag + "h")
            P.emit("vector", lambda v, hi_ap=hi_ap, lo_ap=lo_ap:
                   nc.vector.tensor_tensor(lo_ap, src_ap, hi_ap, ALU.subtract),
                   reads=list(bufs_src) + [B_st[half]], writes=[B_st[half]],
                   tag=tag + "l")

        def phase_ln(i, tag):
            ln_tile(xh(i), [B_x[i]], facc[:, :], [B_facc[0], B_facc[1]])
            tm2fm(facc[:, :], [B_facc[0], B_facc[1]], xlh(i), [B_xlnT[i]], tag)

        wslot_state = {"n": 0}

        def next_wslot(use3=False):
            s = wslot_state["n"]
            wslot_state["n"] = (s + 1) % 2
            return s

        pbank_state = {"n": 0}

        def next_pbank():
            s = pbank_state["n"]
            pbank_state["n"] = (s + 1) % 4
            return s

        HKC = NPT // 2 * 2 * PW

        def load_wpiece(slot, name, pi_):
            for hh in range(2):
                P.emit("sync", lambda s, name=name, pi_=pi_, slot=slot, hh=hh:
                       s.dma_start(
                           out=wbuf[slot][:, hh * HKC:(hh + 1) * HKC],
                           in_=whl[name][pi_, :, hh * HKC:(hh + 1) * HKC]),
                       reads=[B_wsc], writes=[B_w[slot][hh]], tag="wload")

        def proj_piece(wslot, pbank, half, bias_ap, bias_buf=None):
            def mm(tn, pbank=pbank, bias_ap=bias_ap, wslot=wslot, half=half,
                   hh=0):
                last = None
                for dc in range(hh * 4, hh * 4 + 4):
                    sh = st_hi[:, half * D + dc * 128:half * D + (dc + 1) * 128]
                    sl = st_lo[:, half * D + dc * 128:half * D + (dc + 1) * 128]
                    whi_ap = wbuf[wslot][:, dc * 2 * PW:dc * 2 * PW + PW]
                    wlo_ap = wbuf[wslot][:, dc * 2 * PW + PW:(dc + 1) * 2 * PW]
                    last = nc.tensor.matmul(
                        psum[pbank][:, 0:PW], sh, whi_ap,
                        start=(dc == 0), stop=False)
                    last = nc.tensor.matmul(
                        psum[pbank][:, 0:PW], sh, wlo_ap,
                        start=False, stop=False)
                    last = nc.tensor.matmul(
                        psum[pbank][:, 0:PW], sl, whi_ap,
                        start=False,
                        stop=(hh == 1 and dc == NPT - 1 and bias_ap is None))
                if hh == 1 and bias_ap is not None:
                    last = nc.tensor.matmul(
                        psum[pbank][:, 0:PW], ones2[:, :], bias_ap,
                        start=False, stop=True)
                return last
            extra = [bias_buf] if bias_buf is not None else []
            def mmall(tn, mm=mm):
                mm(tn, hh=0)
                return mm(tn, hh=1)
            P.emit("tensor", mmall,
                   reads=[B_st[half], B_w[wslot][0], B_w[wslot][1],
                          B_const] + extra,
                   writes=[B_ps[pbank]], tag="proj")

        def stream(wname, dst_cb, bias_seg, tag, halves=(0, 1), use3=False,
                   halves_outer=False, mid_cb=None):
            def piece(pi_, c0, phalves):
                slot = next_wslot(use3)
                load_wpiece(slot, wname, pi_)
                bias_ap = None
                bias_buf = None
                if bias_seg is not None:
                    bsl = pi_ % 2
                    bias_ap = bias_stage[:, bsl * PW:(bsl + 1) * PW]
                    bias_buf = B_bias[bsl]
                    boff = BSEG[bias_seg] * D + c0
                    P.emit("gpsimd", lambda g, boff=boff, ba=bias_ap:
                           g.dma_start(out=ba, in_=bias_hl[:, boff:boff + PW]),
                           writes=[bias_buf], reads=[B_wsc], tag="bias",
                           dma=True)
                for half in phalves:
                    pb_ = next_pbank()
                    proj_piece(slot, pb_, half, bias_ap, bias_buf=bias_buf)
                    dst_cb(c0, pb_, half)
            if halves_outer:
                for half in halves:
                    for pi_, c0 in enumerate(range(0, D, PW)):
                        piece(pi_, c0, (half,))
                    if half == 0 and mid_cb is not None:
                        mid_cb()
            else:
                for pi_, c0 in enumerate(range(0, D, PW)):
                    piece(pi_, c0, halves)

        def emit_mix(kind, half, carry_ap, B_carry, dst_ap, bufs_dst):
            mu_sl = mu[:, kind * NPT:(kind + 1) * NPT]
            omu_sl = omu[:, kind * NPT:(kind + 1) * NPT]
            x3 = xlh(half).rearrange("p (a c) -> p a c", a=NPT)
            o3 = dst_ap.rearrange("p (a c) -> p a c", a=NPT)
            if half == 0:
                carry_src = [carry_ap[:, a2:a2 + 1] for a2 in range(NPT)]
                carry_reads = [B_carry]
            else:
                x3p = xlh(0).rearrange("p (a c) -> p a c", a=NPT)
                carry_src = [x3p[:, a2, 127:128] for a2 in range(NPT)]
                carry_reads = [B_xlnT[0]]

            def mix1(v):
                last = None
                for a2 in range(NPT):
                    last = nc.vector.tensor_scalar(
                        o3[:, a2, :], x3[:, a2, :], mu_sl[:, a2:a2 + 1], None,
                        ALU.mult)
                return last
            P.emit("vector", mix1, reads=[B_xlnT[half], B_const],
                   writes=list(bufs_dst), tag="mix1")

            def mix2(v):
                last = None
                for a2 in range(NPT):
                    last = nc.vector.scalar_tensor_tensor(
                        o3[:, a2, 1:128], x3[:, a2, 0:127], omu_sl[:, a2:a2 + 1],
                        o3[:, a2, 1:128], ALU.mult, ALU.add)
                    last = nc.vector.scalar_tensor_tensor(
                        o3[:, a2, 0:1], carry_src[a2], omu_sl[:, a2:a2 + 1],
                        o3[:, a2, 0:1], ALU.mult, ALU.add)
                return last
            P.emit("vector", mix2,
                   reads=[B_xlnT[half], B_const] + carry_reads,
                   writes=list(bufs_dst), tag="mix2")

        def ffn(ts, carry_ap, B_carry, tag, ln_done=(False, False),
                v_mid_cb=None):
            mixdst = (prod, q_fm)
            mixbuf = (B_prod, B_qfm)
            for i in range(2):
                if not ln_done[i]:
                    phase_ln(i, f"{tag}lnT{i}")
            for i in range(2):
                emit_mix(0, i, carry_ap, B_carry, mixdst[i][:, :], [mixbuf[i]])
                split_half(mixdst[i][:, :], [mixbuf[i]], i, f"{tag}sxr{i}")

            def rsig_cb(c0, pb_, half):
                P.emit("scalar", lambda s, c0=c0, pb_=pb_, half=half:
                       nc.scalar.activation(
                           a_ch[:, half * D + c0:half * D + c0 + PW],
                           psum[pb_][:, 0:PW], AF.Sigmoid),
                       reads=[B_ps[pb_]], writes=[B_a[half]], tag=f"{tag}rsig")
            stream("r", rsig_cb, None, f"{tag}Wr", use3=True)
            for i in range(2):
                emit_mix(1, i, carry_ap, B_carry, mixdst[i][:, :], [mixbuf[i]])
                split_half(mixdst[i][:, :], [mixbuf[i]], i, f"{tag}sxk{i}")
            P.emit("vector", lambda v: nc.vector.tensor_copy(
                carry_ap[:, :],
                xlh(1).rearrange("p (a c) -> p a c", a=NPT)[:, :, 127]),
                reads=[B_xlnT[1]], writes=[B_carry], tag=f"{tag}carry", small=True)

            relu_t = (prod, facc)
            relu_b = ([B_prod], [B_facc[0], B_facc[1]])

            def krelu_cb(c0, pb_, half):
                P.emit("vector", lambda v, c0=c0, pb_=pb_, half=half:
                       nc.vector.tensor_scalar_max(
                           relu_t[half][:, c0:c0 + PW], psum[pb_][:, 0:PW], 0.0),
                       reads=[B_ps[pb_]], writes=relu_b[half],
                       tag=f"{tag}krelu")
            stream("k", krelu_cb, None, f"{tag}Wk", use3=True)

            for i in range(2):
                def ktr(tn, i=i):
                    last = None
                    for p2 in range(NPT):
                        bank = 6 + (p2 // 4)
                        last = nc.tensor.transpose(
                            psum[bank][:, (p2 % 4) * 128:(p2 % 4) * 128 + 128],
                            relu_t[i][:, p2 * 128:(p2 + 1) * 128], ident[:, :])
                    return last
                P.emit("tensor", ktr, reads=relu_b[i] + [B_const],
                       writes=[B_ps[6], B_ps[7]], tag=f"{tag}ktr{i}")
                P.emit("scalar", lambda s, i=i: nc.scalar.activation(
                    b_ch[:, i * D:i * D + 512], psum[6][:, :], AF.Square),
                    reads=[B_ps[6]], writes=[B_b[i]], tag=f"{tag}ksq0")
                P.emit("scalar", lambda s, i=i: nc.scalar.activation(
                    b_ch[:, i * D + 512:i * D + 1024], psum[7][:, :], AF.Square),
                    reads=[B_ps[7]], writes=[B_b[i]], tag=f"{tag}ksq1")
                split_half(bh(i), [B_b[i]], i, f"{tag}skt{i}")

            def v_cb(c0, pb_, half):
                vt = relu_t[half]
                vb = relu_b[half]
                P.emit("vector",
                       lambda v, c0=c0, pb_=pb_, half=half, vt=vt:
                       nc.vector.tensor_tensor(
                           vt[:, c0:c0 + PW],
                           a_ch[:, half * D + c0:half * D + c0 + PW],
                           psum[pb_][:, 0:PW], ALU.mult),
                       reads=[B_ps[pb_], B_a[half]], writes=vb, tag=f"{tag}vm")
                P.emit("vector",
                       lambda v, c0=c0, half=half, vt=vt:
                       nc.vector.scalar_tensor_tensor(
                           x_ch[:, half * D + c0:half * D + c0 + PW],
                           vt[:, c0:c0 + PW], 0.5,
                           x_ch[:, half * D + c0:half * D + c0 + PW],
                           ALU.mult, ALU.add),
                       reads=vb + [B_x[half]], writes=[B_x[half]], tag=f"{tag}va")
            stream("v", v_cb, None, f"{tag}Wv", use3=True,
                   halves_outer=(v_mid_cb is not None), mid_cb=v_mid_cb)
            if ts[0] == 0 and tag[0] in "AC":
                dump(tag[0] + "_xmid", xh(0), D, [B_x[0]])

        def attn_core(t, half):
            qk3 = qk_cmp[:, :].rearrange("p (h o) -> p h o", h=H)
            for ci, m in enumerate(FAR_M):
                rk = NRANK + ci
                if t < m:
                    P.emit("vector", lambda v, rk=rk, qk3=qk3: nc.vector.memset(
                        qk3[:, :, rk:rk + 1], 0.0),
                        writes=[B_qkf], tag=f"fneg{t}_{ci}", small=True)
                    continue
                sl = (t - m) % RING
                P.emit("vector", lambda v, sl=sl, half=half: nc.vector.tensor_tensor(
                    prod[:, :], ah(half),
                    k_tm[:, sl * D:(sl + 1) * D], ALU.mult),
                    reads=[B_a[half], B_ktm[sl]], writes=[B_prod],
                    tag=f"fqk{t}_{ci}")
                P.emit("vector", lambda v, rk=rk, qk3=qk3: nc.vector.tensor_reduce(
                    qk3[:, :, rk:rk + 1],
                    prod[:, :].rearrange("p (h d) -> p h d", h=H), AX.X, ALU.add),
                    reads=[B_prod], writes=[B_qkf], tag=f"frd{t}_{ci}", small=True)
            tm2fm(ah(half), [B_a[half]], q_fm[:, :], [B_qfm], f"qT{t}")
            if t == 0:
                dump("B_qfm", q_fm[:, :], D, [B_qfm])
            ksl = t % RING
            kfsl = t % KFR
            tm2fm(k_tm[:, ksl * D:(ksl + 1) * D], [B_ktm[ksl]],
                  k_fm[:, kfsl * D:(kfsl + 1) * D], [B_kfm[kfsl]], f"kT{t}")
            cstart = {0: 192, 1: 64}.get(t, 0)
            for h in range(H):
                hp = h // 2
                base = 64 * (h % 2)
                bslot = h % 2
                bank = 2 + (h % 2)

                def bmm(tn, h=h, hp=hp, base=base, bank=bank, t=t):
                    last = None
                    segs = []
                    if t >= 2:
                        segs.append(((t - 2) % KFR, 64, 128, 0))
                    if t >= 1:
                        segs.append(((t - 1) % KFR, 0, 128, 64))
                    segs.append((t % KFR, 0, 128, 192))
                    for (sl, a0, a1, oc) in segs:
                        last = nc.tensor.matmul(
                            psum[bank][:, oc:oc + (a1 - a0)],
                            q_fm[base:base + 64, hp * 128:(hp + 1) * 128],
                            k_fm[base:base + 64,
                                 sl * D + hp * 128 + a0:sl * D + hp * 128 + a1],
                            start=True, stop=True)
                    last = nc.tensor.matmul(
                        psum[bank][:, WIN:WIN + NOFF],
                        q_fm[base:base + 64, hp * 128:(hp + 1) * 128],
                        sedup[base:base + 64, :], start=True, stop=True)
                    return last
                kf_reads = [B_kfm[(t - k) % KFR] for k in range(min(3, t + 1))]
                P.emit("tensor", bmm, reads=[B_qfm, B_const] + kf_reads,
                       writes=[B_ps[bank]], tag=f"band{t}_{h}")
                if t < 2:
                    P.emit("vector", lambda v, bslot=bslot: nc.vector.memset(
                        band_scr[:, bslot * WIN:bslot * WIN + WIN], 0.0),
                        writes=[B_band[bslot]], tag="bz")

                P.emit("scalar", lambda o, bank=bank, bslot=bslot,
                       cstart=cstart: nc.scalar.copy(
                           band_scr[:, bslot * WIN + cstart:bslot * WIN + WIN],
                           psum[bank][:, cstart:WIN]),
                       reads=[B_ps[bank]], writes=[B_band[bslot]],
                       tag=f"bev{t}_{h}")
                P.emit("scalar", lambda o, bank=bank, h=h: nc.scalar.copy(
                    qs_cmp[:, h * NOFF:(h + 1) * NOFF],
                    psum[bank][:, WIN:WIN + NOFF]),
                    reads=[B_ps[bank]], writes=[B_qs[h // 8]],
                    tag=f"qsev{t}_{h}")
                P.emit("gpsimd", lambda g, h=h, bslot=bslot: g.local_scatter(
                    qk_cmp[:, h * NOFF:h * NOFF + NRANK].bitcast(I16),
                    band_scr[:, bslot * WIN:bslot * WIN + WIN].bitcast(I16),
                    idxe[:, :], 128, 2 * NRANK, 2 * WIN),
                    reads=[B_band[bslot], B_const], writes=[B_qk[h // 8]],
                    tag=f"ext{t}_{h}")
            if t == 0:
                dump("B_qk", qk_cmp[:, :], H * NOFF, [B_qk[0], B_qk[1], B_qkf])
                dump("B_qs", qs_cmp[:, :], H * NOFF, [B_qs[0], B_qs[1]])
            pbT = (pbandT[:, :], q_fm[:, 0:PWIN])
            pbT_buf = (B_pbandT, B_qfm)

            def emit_pvmm(h):
                def pvmm(tn, h=h, t=t):
                    bank = 4 + (h // 8)
                    ccol = (h % 8) * 64
                    psl = h % 2
                    last = None
                    nch = 0
                    bank_first = (h == 0 or h == 8)
                    bank_last = (h == 7 or h == 15)
                    nvalid = sum(1 for c_ in range(3) if t - 2 + c_ >= 0)
                    for ch_ in range(3):
                        sl_t = t - 2 + ch_
                        if sl_t < 0:
                            continue
                        sl = sl_t % RING
                        last = nc.tensor.matmul(
                            psum[bank][:, ccol:ccol + 64],
                            pbT[psl][:, ch_ * 128:(ch_ + 1) * 128],
                            v_tm[:, sl * D + h * HD:sl * D + (h + 1) * HD],
                            start=(bank_first and nch == 0),
                            stop=(bank_last and nch == nvalid - 1),
                            skip_group_check=True)
                        nch += 1
                    return last
                vt_reads = [B_vtm[(t - 2 + c_) % RING] for c_ in range(3)
                            if t - 2 + c_ >= 0]
                P.emit("tensor", pvmm, reads=[pbT_buf[h % 2]] + vt_reads,
                       writes=[B_ps[4 + (h // 8)]], tag=f"pv{t}_{h}")

            pend = None
            for grp in range(2):
                hg0 = grp * 8
                cs = hg0 * NOFF
                ce = cs + 8 * NOFF
                pg = p_cmp[:, cs:ce]
                lgs = pg.rearrange("p (h o) -> p h o", h=8)
                P.emit("vector", lambda v, pg=pg, cs=cs, ce=ce:
                       nc.vector.tensor_tensor(
                           pg, qk_cmp[:, cs:ce], qs_cmp[:, cs:ce], ALU.mult),
                       reads=[B_qk[grp], B_qkf, B_qs[grp]], writes=[B_p[grp]],
                       tag=f"lg1_{grp}")
                P.emit("vector", lambda v, pg=pg, cs=cs, ce=ce:
                       nc.vector.tensor_tensor(
                           pg, pg, qk_cmp[:, cs:ce], ALU.add),
                       reads=[B_p[grp], B_qk[grp], B_qkf], writes=[B_p[grp]],
                       tag=f"lg2_{grp}")
                P.emit("vector", lambda v, pg=pg, cs=cs, ce=ce:
                       nc.vector.scalar_tensor_tensor(
                           pg, pg, INV, pbc[:, cs:ce], ALU.mult, ALU.add),
                       reads=[B_p[grp], B_const], writes=[B_p[grp]],
                       tag=f"lg3_{grp}")
                for ci, m in enumerate(FAR_M):
                    if t < m:
                        rk = NRANK + ci
                        P.emit("vector", lambda v, rk=rk, lgs=lgs:
                               nc.vector.memset(lgs[:, :, rk:rk + 1], NEG),
                               writes=[B_p[grp]], tag=f"lgneg{t}_{ci}_{grp}",
                               small=True)
                if t == 0:
                    P.emit("vector", lambda v, lgs=lgs: nc.vector.tensor_tensor(
                        lgs[:, :, 0:NRANK], lgs[:, :, 0:NRANK],
                        mask0[:, :].unsqueeze(1).broadcast_to([128, 8, NRANK]),
                        ALU.add),
                        reads=[B_p[grp], B_const], writes=[B_p[grp]],
                        tag=f"lgmask_{grp}")
                elif t == 1:
                    P.emit("vector", lambda v, pg=pg: nc.vector.memset(
                        pg[0:64, :].rearrange(
                            "p (h o) -> p h o", h=8)[:, :, NRANK - 1:NRANK], NEG),
                        reads=[B_p[grp]], writes=[B_p[grp]],
                        tag=f"lgmask1_{grp}", small=True)
                if t == 0 and grp == 1:
                    dump("B_lg", p_cmp[:, :], H * NOFF, [B_p[0], B_p[1]])
                mx = red16[:, hg0:hg0 + 8]
                sm = red16[:, H + hg0:H + hg0 + 8]
                rc = rec16[:, hg0:hg0 + 8]
                P.emit("vector", lambda v, lgs=lgs, mx=mx: nc.vector.tensor_reduce(
                    mx.rearrange("p (h o) -> p h o", o=1), lgs, AX.X, ALU.max),
                    reads=[B_p[grp]], writes=[B_red[grp]], tag=f"smax{grp}",
                    small=True)
                P.emit("vector", lambda v, lgs=lgs, mx=mx: nc.vector.tensor_tensor(
                    lgs, lgs,
                    mx.rearrange("p (h o) -> p h o", o=1).broadcast_to(
                        [128, 8, NOFF]), ALU.subtract),
                    reads=[B_p[grp], B_red[grp]], writes=[B_p[grp]],
                    tag=f"ssub{grp}")
                P.emit("scalar", lambda s, pg=pg: nc.scalar.activation(
                    pg, pg, AF.Exp),
                    reads=[B_p[grp]], writes=[B_p[grp]], tag=f"sexp{grp}")
                P.emit("vector", lambda v, lgs=lgs, sm=sm: nc.vector.tensor_reduce(
                    sm.rearrange("p (h o) -> p h o", o=1), lgs, AX.X, ALU.add),
                    reads=[B_p[grp]], writes=[B_red[grp]], tag=f"ssum{grp}",
                    small=True)
                P.emit("vector", lambda v, rc=rc, sm=sm: nc.vector.reciprocal(
                    rc, sm),
                    reads=[B_red[grp]], writes=[B_rec[grp]], tag=f"srcp1{grp}",
                    small=True)
                P.emit("vector", lambda v, rc=rc, hg0=hg0: nc.vector.tensor_tensor(
                    rc, rc, gain16[:, hg0:hg0 + 8], ALU.mult),
                    reads=[B_rec[grp], B_const], writes=[B_rec[grp]],
                    tag=f"srcp2{grp}", small=True)
                P.emit("vector", lambda v, lgs=lgs, rc=rc: nc.vector.tensor_tensor(
                    lgs, lgs,
                    rc.rearrange("p (h o) -> p h o", o=1).broadcast_to(
                        [128, 8, NOFF]), ALU.mult),
                    reads=[B_p[grp], B_rec[grp]], writes=[B_p[grp]],
                    tag=f"smul{grp}")
                if t == 0 and grp == 1:
                    dump("B_p", p_cmp[:, :], H * NOFF, [B_p[0], B_p[1]])
                first = True
                facc_g = facc[:, grp * 512:(grp + 1) * 512]
                prod_g = prod[:, grp * 512:(grp + 1) * 512]
                for ci, m in enumerate(FAR_M):
                    rk = NRANK + ci
                    if t < m:
                        continue
                    sl = (t - m) % RING
                    pb_ap = lgs[:, :, rk:rk + 1].broadcast_to([128, 8, HD])
                    v_g = v_tm[:, sl * D + grp * 512:sl * D + (grp + 1) * 512]
                    if first:
                        P.emit("vector", lambda v, pb_ap=pb_ap, v_g=v_g,
                               facc_g=facc_g: nc.vector.tensor_tensor(
                                   facc_g.rearrange("p (h d) -> p h d", h=8),
                                   v_g.rearrange("p (h d) -> p h d", h=8),
                                   pb_ap, ALU.mult),
                               reads=[B_p[grp], B_vtm[sl]],
                               writes=[B_facc[grp]], tag=f"fpv{t}_{ci}_{grp}")
                        first = False
                    else:
                        P.emit("vector", lambda v, pb_ap=pb_ap, v_g=v_g,
                               prod_g=prod_g: nc.vector.tensor_tensor(
                                   prod_g.rearrange("p (h d) -> p h d", h=8),
                                   v_g.rearrange("p (h d) -> p h d", h=8),
                                   pb_ap, ALU.mult),
                               reads=[B_p[grp], B_vtm[sl]], writes=[B_prod],
                               tag=f"fpm{t}_{ci}_{grp}")
                        P.emit("vector", lambda v, facc_g=facc_g, prod_g=prod_g:
                               nc.vector.tensor_tensor(
                                   facc_g, facc_g, prod_g, ALU.add),
                               reads=[B_prod, B_facc[grp]],
                               writes=[B_facc[grp]], tag=f"fpa{t}_{ci}_{grp}")
                if first:
                    P.emit("vector", lambda v, facc_g=facc_g: nc.vector.memset(
                        facc_g, 0.0),
                        writes=[B_facc[grp]], tag=f"fz{grp}")
                for h in range(hg0, hg0 + 8):
                    psl = h % 2
                    tbank = 6 + psl
                    P.emit("gpsimd", lambda g, h=h, psl=psl: g.local_scatter(
                        pband[psl][:, :].bitcast(I16),
                        p_cmp[:, h * NOFF:h * NOFF + NRANK].bitcast(I16),
                        idxb[:, :], 128, 2 * PWIN, 2 * NRANK),
                        reads=[B_p[grp], B_const], writes=[B_pband[psl]],
                        tag=f"bld{t}_{h}")

                    def ptr(tn, psl=psl, tbank=tbank):
                        last = None
                        for ch_ in range(3):
                            last = nc.tensor.transpose(
                                psum[tbank][:, ch_ * 128:(ch_ + 1) * 128],
                                pband[psl][:, ch_ * 128:(ch_ + 1) * 128],
                                ident[:, :])
                        return last
                    P.emit("tensor", ptr, reads=[B_pband[psl], B_const],
                           writes=[B_ps[tbank]], tag=f"ptr{t}_{h}")
                    if pend is not None:
                        emit_pvmm(pend)
                    P.emit("scalar", lambda o, psl=psl, tbank=tbank:
                           nc.scalar.copy(
                               pbT[psl][:, :], psum[tbank][:, 0:PWIN]),
                           reads=[B_ps[tbank]], writes=[pbT_buf[psl]],
                           tag=f"pte{t}_{h}")
                    pend = h
            emit_pvmm(pend)
            for hf in range(2):
                P.emit("vector", lambda v, hf=hf: nc.vector.scalar_tensor_tensor(
                    prod[:, hf * 512:(hf + 1) * 512],
                    psum[4 + hf][:, :], 1.0,
                    facc[:, hf * 512:(hf + 1) * 512], ALU.mult, ALU.add),
                    reads=[B_ps[4 + hf], B_facc[hf]], writes=[B_prod],
                    tag=f"goadd{t}_{hf}")
            P.emit("vector", lambda v, half=half: nc.vector.tensor_tensor(
                prod[:, :], prod[:, :], bh(half), ALU.mult),
                reads=[B_prod, B_b[half]], writes=[B_prod], tag=f"gomul{t}")
            tm2fm(prod[:, :], [B_prod], xlh(half), [B_xlnT[half]], f"goT{t}")
            if t == 0:
                dump("B_go", xlh(0), D, [B_xlnT[0]])

        def attn(ts, ln_done=(False, False), o_mid_cb=None):
            for i in range(2):
                if ln_done[i]:
                    continue
                phase_ln(i, f"b{ts[i]}lnT")
                split_half(xlh(i), [B_xlnT[i]], i, f"b{ts[i]}sln")

            def qev_cb(c0, pb_, half):
                P.emit("scalar", lambda s, c0=c0, pb_=pb_, half=half:
                       nc.scalar.copy(
                           a_ch[:, half * D + c0:half * D + c0 + PW],
                           psum[pb_][:, 0:PW]),
                       reads=[B_ps[pb_]], writes=[B_a[half]], tag="qev")
            stream("q", qev_cb, "q", "Wq")

            def gev_cb(c0, pb_, half):
                P.emit("scalar", lambda s, c0=c0, pb_=pb_, half=half:
                       nc.scalar.activation(
                           b_ch[:, half * D + c0:half * D + c0 + PW],
                           psum[pb_][:, 0:PW], AF.Sigmoid),
                       reads=[B_ps[pb_]], writes=[B_b[half]], tag="gev")
            stream("g", gev_cb, "g", "Wg")

            for i, t in enumerate(ts):
                ksl = t % RING

                def kev_cb(c0, pb_, half, ksl=ksl):
                    P.emit("scalar", lambda s, c0=c0, pb_=pb_, ksl=ksl:
                           nc.scalar.copy(
                               k_tm[:, ksl * D + c0:ksl * D + c0 + PW],
                               psum[pb_][:, 0:PW]),
                           reads=[B_ps[pb_]], writes=[B_ktm[ksl]], tag="kev")
                stream("k2", kev_cb, "k2", f"Wk2_{t}", halves=(i,))

                def vev_cb(c0, pb_, half, ksl=ksl):
                    P.emit("scalar", lambda s, c0=c0, pb_=pb_, ksl=ksl:
                           nc.scalar.copy(
                               v_tm[:, ksl * D + c0:ksl * D + c0 + PW],
                               psum[pb_][:, 0:PW]),
                           reads=[B_ps[pb_]], writes=[B_vtm[ksl]], tag="vev")
                stream("v2", vev_cb, "v2", f"Wv2_{t}", halves=(i,))
                if t == 0:
                    dump("B_q", ah(0), D, [B_a[0]])
                    dump("B_k", k_tm[:, 0:D], D, [B_ktm[0]])
                    dump("B_v", v_tm[:, 0:D], D, [B_vtm[0]])
                    dump("B_gate", bh(0), D, [B_b[0]])
                attn_core(t, i)

            for i in range(2):
                split_half(xlh(i), [B_xlnT[i]], i, f"b{ts[i]}sgo")

            def oacc_cb(c0, pb_, half):
                x_sl = x_ch[:, half * D + c0:half * D + c0 + PW]
                P.emit("vector", lambda v, pb_=pb_, x_sl=x_sl, half=half:
                       nc.vector.tensor_tensor(
                           x_sl, x_sl, psum[pb_][:, 0:PW], ALU.add),
                       reads=[B_ps[pb_], B_x[half]], writes=[B_x[half]],
                       tag="oacc")
            stream("o", oacc_cb, "o", "Wo",
                   halves_outer=(o_mid_cb is not None), mid_cb=o_mid_cb)
            if ts[0] == 0:
                dump("B_x", xh(0), D, [B_x[0]])

        assert n_tiles % 2 == 0
        for c in range(n_tiles // 2):
            ts = (2 * c, 2 * c + 1)
            for i, t in enumerate(ts):
                P.emit("sync", lambda s, t=t, i=i: s.dma_start(
                    out=xh(i), in_=x_in[t * 128:(t + 1) * 128, :]),
                    writes=[B_x[i]], tag=f"xload{t}")

            def attn_pre0(ts=ts):
                phase_ln(0, f"b{ts[0]}lnT")
                split_half(xlh(0), [B_xlnT[0]], 0, f"b{ts[0]}sln")

            def c_ln0(c=c):
                phase_ln(0, f"C{c}_lnT0")

            def store0(ts=ts):
                P.emit("sync", lambda s, t=ts[0]: s.dma_start(
                    out=y_out[t * 128:(t + 1) * 128, :], in_=xh(0)),
                    reads=[B_x[0]], writes=[B_out[ts[0]]], tag=f"store{ts[0]}")
            ffn(ts, carryA[:, :], B_carryA, f"A{c}_", v_mid_cb=attn_pre0)
            attn(ts, ln_done=(True, False), o_mid_cb=c_ln0)
            ffn(ts, carryC[:, :], B_carryC, f"C{c}_", ln_done=(True, False),
                v_mid_cb=store0)
            P.emit("sync", lambda s, t=ts[1]: s.dma_start(
                out=y_out[t * 128:(t + 1) * 128, :], in_=xh(1)),
                reads=[B_x[1]], writes=[B_out[ts[1]]], tag=f"store{ts[1]}")

        P.emit("tensor", lambda tn: nc.tensor.nop(),
               reads=[B_out[t] for t in range(n_tiles)], tag="fence")

        runner = P.make_runner(sems)
        with nc.Block() as block:
            block.sync(lambda o: runner(o, "sync"))
            block.tensor(lambda o: runner(o, "tensor"))
            block.vector(lambda o: runner(o, "vector"))
            block.scalar(lambda o: runner(o, "scalar"))
            block.gpsimd(lambda o: runner(o, "gpsimd"))

    lower_extended_insts(nc)
    return nc


def _prep_inputs(inputs):
    consts = _host_consts()
    pos_bias = np.asarray(inputs["L0_attn_pos_bias"], np.float32)
    scale_embed = np.asarray(inputs["L0_attn_scale_embed"], np.float32)
    if_gain = np.asarray(inputs["L0_attn_if_gain"], np.float32)
    mu_r = np.asarray(inputs["L0_ffn_mu_r"], np.float32)
    mu_k = np.asarray(inputs["L0_ffn_mu_k"], np.float32)
    sedup = np.empty((128, NOFF), np.float32)
    sedup[0:64] = scale_embed.T
    sedup[64:128] = scale_embed.T
    pbconst = np.ascontiguousarray(
        np.broadcast_to(pos_bias.T.reshape(1, H * NOFF), (128, H * NOFF)))
    gain16 = np.ascontiguousarray(
        np.broadcast_to(if_gain.reshape(1, H), (128, H)))
    mu_in = np.empty((128, 2 * NPT), np.float32)
    mu_in[:, 0:NPT] = mu_r.reshape(NPT, 128).T
    mu_in[:, NPT:2 * NPT] = mu_k.reshape(NPT, 128).T
    common = dict(
        qkv_wT=np.ascontiguousarray(np.asarray(inputs["L0_attn_qkv_w"], np.float32).T),
        gate_wT=np.ascontiguousarray(np.asarray(inputs["L0_attn_gate_w"], np.float32).T),
        out_wT=np.ascontiguousarray(np.asarray(inputs["L0_attn_out_w"], np.float32).T),
        WrT=np.ascontiguousarray(np.asarray(inputs["L0_ffn_W_r"], np.float32).T),
        WkT=np.ascontiguousarray(np.asarray(inputs["L0_ffn_W_k"], np.float32).T),
        WvT=np.ascontiguousarray(np.asarray(inputs["L0_ffn_W_v"], np.float32).T),
        qkv_b=np.asarray(inputs["L0_attn_qkv_b"], np.float32).reshape(1, -1),
        gate_b=np.asarray(inputs["L0_attn_gate_b"], np.float32).reshape(1, -1),
        out_b=np.asarray(inputs["L0_attn_out_b"], np.float32).reshape(1, -1),
        sedup=sedup, pbconst=pbconst, gain16=gain16, mu_in=mu_in,
        identity=consts["identity"],
        idx_ext=consts["idx_ext"], idx_bld=consts["idx_bld"],
        mask0=consts["mask0"],
    )
    return common


def kernel(**inputs):
    x = np.ascontiguousarray(np.asarray(inputs["L0_attn_x"], np.float32))
    B = x.shape[0]
    common = _prep_inputs(inputs)
    nc = build_nc(n_tiles=32)
    in_maps = [dict(common, x=np.ascontiguousarray(x[b])) for b in range(B)]
    res = run_bass_kernel_spmd(nc, in_maps, core_ids=list(range(B)))
    return np.stack([res.results[b]["y"] for b in range(B)], axis=0)


# version 07
# speedup vs baseline: 1.0603x; 1.0603x over previous
import numpy as np
from contextlib import ExitStack

import concourse.bass as bass
import concourse.mybir as mybir
from concourse.bass_utils import run_bass_kernel_spmd
from concourse import library_config
from concourse.library_overlay import lower_extended_insts

F32 = mybir.dt.float32
BF16 = mybir.dt.bfloat16
I16 = mybir.dt.int16
AF = mybir.ActivationFunctionType
ALU = mybir.AluOpType
AX = mybir.AxisListType

D = 1024
H = 16
HD = 64
NPT = 8
NEG = -1.0e30
EPS = 1e-5
INV = 0.125

OFFSETS = sorted(set(range(33)) | {48, 64, 96, 128, 192, 256, 384, 512, 768, 1024, 1536})
NEAR = [o for o in OFFSETS if o <= 192]
FAR = [o for o in OFFSETS if o > 192]
FAR_M = [o // 128 for o in FAR]
NRANK = len(NEAR)
NOFF = len(OFFSETS)
RING = 13
KFR = 3
WIN = 320
PWIN = 384
PW = 256
ENGINES = ("tensor", "vector", "scalar", "gpsimd", "sync")


def _host_consts():
    c = {}
    c["identity"] = np.eye(128, dtype=np.float32)
    idx_ext = np.full((128, 2 * WIN), -1, np.int16)
    for n in range(128):
        for r, o in enumerate(NEAR):
            col = n + 192 - o
            if 0 <= col < WIN:
                idx_ext[n, 2 * col] = 2 * r
                idx_ext[n, 2 * col + 1] = 2 * r + 1
    c["idx_ext"] = idx_ext
    idx_bld = np.full((128, 2 * NRANK), -1, np.int16)
    for n in range(128):
        for r, o in enumerate(NEAR):
            w = n - o + 256
            idx_bld[n, 2 * r] = 2 * w
            idx_bld[n, 2 * r + 1] = 2 * w + 1
    c["idx_bld"] = idx_bld
    for j in (0, 1):
        m = np.zeros((128, NRANK), np.float32)
        for n in range(128):
            for r, o in enumerate(NEAR):
                if 128 * j + n < o:
                    m[n, r] = NEG
        c[f"mask{j}"] = m
    return c


class Buf:
    __slots__ = ("last_w", "readers", "name")

    def __init__(self, name=""):
        self.last_w = None
        self.readers = []
        self.name = name


class Prog:
    def __init__(self, serial=False, same_engine_barriers=True):
        self.ops = {e: [] for e in ENGINES}
        self.serial = serial
        self.seb = same_engine_barriers
        self.last_tok = None

    def emit(self, eng, fn, reads=(), writes=(), tag="", small=False, dma=False):
        is_dma = dma or eng == "sync"
        deps = []
        if self.serial and self.last_tok is not None:
            deps.append(self.last_tok)
        for b in reads:
            if b.last_w is not None:
                deps.append(b.last_w)
        for b in writes:
            deps.extend(b.readers)
            if b.last_w is not None:
                deps.append(b.last_w)
        uniq = {}
        for (pe, pi, pd) in deps:
            if pe == eng and not pd and not self.serial:
                if eng in ("tensor", "gpsimd"):
                    continue
                if (not self.ops[pe][pi].get("small") and not self.seb):
                    continue
            uniq[(pe, pi)] = (pe, pi, pd)
        idx = len(self.ops[eng])
        self.ops[eng].append({"fn": fn, "deps": list(uniq.values()),
                              "inc": is_dma, "tag": tag, "small": small,
                              "step": 16 if is_dma else 1})
        for (pe, pi, _d) in uniq.values():
            self.ops[pe][pi]["inc"] = True
        tok = (eng, idx, is_dma)
        self.last_tok = tok
        for b in reads:
            b.readers.append(tok)
        for b in writes:
            b.last_w = tok
            b.readers = []
        return tok

    def make_runner(self, sems):
        prefix = {}
        for e in ENGINES:
            cnt = 0
            pr = []
            for op in self.ops[e]:
                if op["inc"]:
                    cnt += op["step"]
                pr.append(cnt)
            prefix[e] = pr

        def run_engine(eng_obj, e):
            waited = {}
            for op in self.ops[e]:
                for (pe, pi, _d) in op["deps"]:
                    val = prefix[pe][pi]
                    if waited.get(pe, 0) < val:
                        eng_obj.wait_ge(sems[pe], val)
                        waited[pe] = val
                last = op["fn"](eng_obj)
                if op["inc"]:
                    assert last is not None, f"op returned None: {op['tag']}"
                    last.then_inc(sems[e], op["step"])
        return run_engine


def build_nc(n_tiles=32, dumps=(), serial=False, seb=True):
    nc = bass.Bass("TRN2", target_bir_lowering=False)
    P = Prog(serial=serial, same_engine_barriers=seb)
    dump_regs = {}

    x_in = nc.dram_tensor("x", [4096, D], F32, kind="ExternalInput").ap()
    y_out = nc.dram_tensor("y", [4096, D], F32, kind="ExternalOutput").ap()
    qkv_wT = nc.dram_tensor("qkv_wT", [D, 3 * D], F32, kind="ExternalInput").ap()
    gate_wT = nc.dram_tensor("gate_wT", [D, D], F32, kind="ExternalInput").ap()
    out_wT = nc.dram_tensor("out_wT", [D, D], F32, kind="ExternalInput").ap()
    WrT_d = nc.dram_tensor("WrT", [D, D], F32, kind="ExternalInput").ap()
    WkT_d = nc.dram_tensor("WkT", [D, D], F32, kind="ExternalInput").ap()
    WvT_d = nc.dram_tensor("WvT", [D, D], F32, kind="ExternalInput").ap()
    qkv_b = nc.dram_tensor("qkv_b", [1, 3 * D], F32, kind="ExternalInput").ap()
    gate_b = nc.dram_tensor("gate_b", [1, D], F32, kind="ExternalInput").ap()
    out_b = nc.dram_tensor("out_b", [1, D], F32, kind="ExternalInput").ap()
    sedup_in = nc.dram_tensor("sedup", [128, NOFF], F32, kind="ExternalInput").ap()
    pb_in = nc.dram_tensor("pbconst", [128, H * NOFF], F32, kind="ExternalInput").ap()
    gain_in = nc.dram_tensor("gain16", [128, H], F32, kind="ExternalInput").ap()
    mu_in = nc.dram_tensor("mu_in", [128, 2 * NPT], F32, kind="ExternalInput").ap()
    ident_in = nc.dram_tensor("identity", [128, 128], F32, kind="ExternalInput").ap()
    idxe_in = nc.dram_tensor("idx_ext", [128, 2 * WIN], I16, kind="ExternalInput").ap()
    idxb_in = nc.dram_tensor("idx_bld", [128, 2 * NRANK], I16, kind="ExternalInput").ap()
    mask0_in = nc.dram_tensor("mask0", [128, NRANK], F32, kind="ExternalInput").ap()

    WSRC = {
        "q": qkv_wT[:, 0:D], "k2": qkv_wT[:, D:2 * D], "v2": qkv_wT[:, 2 * D:3 * D],
        "g": gate_wT, "o": out_wT, "r": WrT_d, "k": WkT_d, "v": WvT_d,
    }
    whl = {n: nc.dram_tensor(f"whl_{n}", [4, 128, NPT * 2 * PW], BF16,
                             kind="Internal").ap() for n in WSRC}
    bias_hl = nc.dram_tensor("bias_hl", [2, 5 * D], BF16, kind="Internal").ap()
    BSEG = {"q": 0, "k2": 1, "v2": 2, "g": 3, "o": 4}
    BSRC = {"q": qkv_b[:, 0:D], "k2": qkv_b[:, D:2 * D],
            "v2": qkv_b[:, 2 * D:3 * D], "g": gate_b, "o": out_b}

    with ExitStack() as stack:
        e = stack.enter_context

        def sb(name, shape, dt=F32):
            return e(nc.sbuf_tensor(name, shape, dt))

        k_tm = sb("k_tm", [128, RING * D])
        v_tm = sb("v_tm", [128, RING * D])
        k_fm = sb("k_fm", [128, KFR * D])
        CT = 2
        x_ch = sb("x_ch", [128, CT * D])
        xlnT_ch = sb("xlnT_ch", [128, CT * D])
        a_ch = sb("a_ch", [128, CT * D])
        b_ch = sb("b_ch", [128, CT * D])
        wbuf = [sb(f"wbuf{i}", [128, NPT * 2 * PW], BF16) for i in range(2)]
        st_hi = sb("st_hi", [128, CT * D], BF16)
        st_lo = sb("st_lo", [128, CT * D], BF16)
        q_fm = sb("q_fm", [128, D])
        band_scr = sb("band_scr", [128, 2 * WIN])
        qslab = sb("qslab", [128, 3 * H * NOFF])
        qs_cmp = qslab[:, 0:H * NOFF]
        qk_cmp = qslab[:, H * NOFF:2 * H * NOFF]
        p_cmp = qslab[:, 2 * H * NOFF:3 * H * NOFF]

        prod = sb("prod", [128, D])
        facc = sb("facc", [128, D])
        pband = [sb(f"pband{i}", [128, PWIN]) for i in range(2)]
        pbandT = sb("pbandT", [128, PWIN])
        stat = sb("stat", [128, 8])
        red16 = sb("red16", [128, 2 * H])
        rec16 = sb("rec16", [128, H])
        carryA = sb("carryA", [128, NPT])
        carryC = sb("carryC", [128, NPT])
        sedup = sb("sedup_sb", [128, NOFF])
        pbc = sb("pbc_sb", [128, H * NOFF])
        gain16 = sb("gain16_sb", [128, H])
        mu = sb("mu_sb", [128, 2 * NPT])
        omu = sb("omu_sb", [128, 2 * NPT])
        ident = sb("ident_sb", [128, 128])
        ones2 = sb("ones2_sb", [2, 128], BF16)
        bias_stage = sb("bias_stage", [2, 2 * PW], BF16)
        idxe = sb("idxe_sb", [128, 2 * WIN], I16)
        idxb = sb("idxb_sb", [128, 2 * NRANK], I16)
        mask0 = sb("mask0_sb", [128, NRANK])

        psum = [e(nc.psum_tensor(f"ps{i}", [128, 512], F32)) for i in range(8)]
        sems = {en: e(nc.semaphore(f"sem_{en}")) for en in ENGINES}

        B_ktm = [Buf(f"ktm{s}") for s in range(RING)]
        B_vtm = [Buf(f"vtm{s}") for s in range(RING)]
        B_kfm = [Buf(f"kfm{s}") for s in range(KFR)]
        B_x = [Buf(f"x{i}") for i in range(2)]
        B_xlnT = [Buf(f"xlnT{i}") for i in range(2)]
        B_a = [Buf(f"a{i}") for i in range(2)]
        B_b = [Buf(f"b{i}") for i in range(2)]
        B_w = [[Buf(f"w{i}_{hh}") for hh in range(2)] for i in range(2)]
        B_st = [Buf(f"st{i}") for i in range(2)]
        B_qfm = Buf("qfm")
        B_band = [Buf(f"band{i}") for i in range(2)]
        B_qs = [Buf(f"qs{g}") for g in range(2)]
        B_qk = [Buf(f"qk{g}") for g in range(2)]
        B_qkf = Buf("qkf")
        B_p = [Buf(f"p{g}") for g in range(2)]
        B_prod = Buf("prod")
        B_facc = [Buf(f"facc{g}") for g in range(2)]
        B_pband = [Buf(f"pband{i}") for i in range(2)]
        B_pbandT = Buf("pbandT")
        B_stat = Buf("stat")
        B_red = [Buf(f"red{g}") for g in range(2)]
        B_rec = [Buf(f"rec{g}") for g in range(2)]
        B_carryA = Buf("carryA")
        B_carryC = Buf("carryC")
        B_ps = [Buf(f"ps{i}") for i in range(8)]
        B_const = Buf("const")
        B_bias = [Buf("bias0"), Buf("bias1")]
        B_wsc = Buf("wscratch")
        B_out = [Buf(f"out{t}") for t in range(n_tiles)]

        def xh(i):
            return x_ch[:, i * D:(i + 1) * D]

        def xlh(i):
            return xlnT_ch[:, i * D:(i + 1) * D]

        def ah(i):
            return a_ch[:, i * D:(i + 1) * D]

        def bh(i):
            return b_ch[:, i * D:(i + 1) * D]

        def dump(tag, src_ap, cols, reads):
            if tag not in dumps or tag in dump_regs:
                return
            t_ = nc.dram_tensor("dbg_" + tag, [128, cols], F32,
                                kind="ExternalOutput").ap()
            dump_regs[tag] = True
            P.emit("sync", lambda s, t_=t_, src_ap=src_ap: s.dma_start(
                out=t_, in_=src_ap), reads=list(reads), tag="dump_" + tag)

        for dst, src in ((sedup[:, :], sedup_in), (pbc[:, :], pb_in),
                         (gain16[:, :], gain_in), (mu[:, :], mu_in),
                         (ident[:, :], ident_in),
                         (idxe[:, :], idxe_in),
                         (idxb[:, :], idxb_in), (mask0[:, :], mask0_in)):
            P.emit("sync", lambda s, d=dst, so=src: s.dma_start(out=d, in_=so),
                   writes=[B_const], tag="cload")

        def _loadlib(g):
            g.load_library(library_config.local_scatter)
            return g.nop()
        P.emit("gpsimd", _loadlib, tag="loadlib")
        P.emit("vector", lambda v: nc.vector.tensor_scalar(
            omu[:, :], mu[:, :], -1.0, 1.0, ALU.mult, ALU.add),
            reads=[B_const], writes=[B_const], tag="omu", small=True)
        for t_, bufs in ((k_tm, B_ktm), (v_tm, B_vtm), (k_fm, B_kfm)):
            P.emit("vector", lambda v, tt=t_: nc.vector.memset(tt[:, :], 0.0),
                   writes=list(bufs), tag="zring")
        P.emit("vector", lambda v: nc.vector.memset(carryA[:, :], 0.0),
               writes=[B_carryA], tag="zcA", small=True)
        P.emit("vector", lambda v: nc.vector.memset(stat[:, 6:7], float(EPS)),
               writes=[B_stat], tag="eps", small=True)
        P.emit("vector", lambda v: nc.vector.memset(ones2[:, :], 1.0),
               writes=[B_const], tag="ones2", small=True)
        P.emit("vector", lambda v: nc.vector.memset(carryC[:, :], 0.0),
               writes=[B_carryC], tag="zcC", small=True)

        for name in WSRC:
            src3 = WSRC[name].rearrange("(a p) c -> p a c", p=128)
            for pi_ in range(4):
                c0 = pi_ * PW
                xv = x_ch[:, :].rearrange("p (a c) -> p a c", a=NPT)
                P.emit("sync", lambda s, src3=src3, c0=c0, xv=xv: s.dma_start(
                    out=xv, in_=src3[:, :, c0:c0 + PW]),
                    writes=[B_x[0], B_x[1]], tag="wsp_ld")
                P.emit("scalar", lambda s: nc.scalar.copy(
                    st_hi[:, :], x_ch[:, :]),
                    reads=[B_x[0], B_x[1]], writes=[B_st[0], B_st[1]],
                    tag="wsp_hi")
                P.emit("vector", lambda v: nc.vector.tensor_tensor(
                    st_lo[:, :], x_ch[:, :], st_hi[:, :], ALU.subtract),
                    reads=[B_x[0], B_x[1], B_st[0], B_st[1]],
                    writes=[B_st[0], B_st[1]], tag="wsp_lo")
                w5 = whl[name][pi_, :, :].rearrange(
                    "p (a h c) -> p a h c", a=NPT, h=2)
                P.emit("sync", lambda s, w5=w5: s.dma_start(
                    out=w5[:, :, 0, :],
                    in_=st_hi[:, :].rearrange("p (a c) -> p a c", a=NPT)),
                    reads=[B_st[0], B_st[1]], writes=[B_wsc], tag="wsp_sh")
                P.emit("gpsimd", lambda g, w5=w5: g.dma_start(
                    out=w5[:, :, 1, :],
                    in_=st_lo[:, :].rearrange("p (a c) -> p a c", a=NPT)),
                    reads=[B_st[0], B_st[1]], writes=[B_wsc], tag="wsp_sl",
                    dma=True)

        for bname, bsrc in BSRC.items():
            seg = BSEG[bname]
            P.emit("sync", lambda s, bsrc=bsrc: s.dma_start(
                out=prod[0:1, :], in_=bsrc),
                writes=[B_prod], tag="bsp_ld")
            P.emit("scalar", lambda s: nc.scalar.copy(
                st_hi[0:1, 0:D], prod[0:1, :]),
                reads=[B_prod], writes=[B_st[0]], tag="bsp_hi", small=True)
            P.emit("vector", lambda v: nc.vector.tensor_tensor(
                st_lo[0:1, 0:D], prod[0:1, :], st_hi[0:1, 0:D], ALU.subtract),
                reads=[B_prod, B_st[0]], writes=[B_st[0]], tag="bsp_lo",
                small=True)
            P.emit("sync", lambda s, seg=seg: s.dma_start(
                out=bias_hl[0:1, seg * D:(seg + 1) * D], in_=st_hi[0:1, 0:D]),
                reads=[B_st[0]], writes=[B_wsc], tag="bsp_sh")
            P.emit("sync", lambda s, seg=seg: s.dma_start(
                out=bias_hl[1:2, seg * D:(seg + 1) * D], in_=st_lo[0:1, 0:D]),
                reads=[B_st[0]], writes=[B_wsc], tag="bsp_sl")

        def ln_tile(x_ap, bufs_x, out_ap, bufs_out):
            s_sum = stat[:, 0:1]
            s_sq = stat[:, 1:2]
            s_mean = stat[:, 2:3]
            s_var = stat[:, 3:4]
            s_rstd = stat[:, 4:5]
            s_nmr = stat[:, 5:6]
            s_eps = stat[:, 6:7]
            P.emit("vector", lambda v: nc.vector.reduce_sum(s_sum, x_ap, AX.X),
                   reads=list(bufs_x), writes=[B_stat], tag="ln_sum", small=True)
            P.emit("scalar", lambda s: nc.scalar.activation(
                prod[:, :], x_ap, AF.Square, accum_out=s_sq),
                reads=list(bufs_x), writes=[B_prod, B_stat], tag="ln_sq", small=True)
            P.emit("vector", lambda v: nc.vector.tensor_scalar(
                s_mean, s_sum, 1.0 / D, None, ALU.mult),
                reads=[B_stat], writes=[B_stat], tag="ln_v1", small=True)
            P.emit("vector", lambda v: nc.vector.tensor_tensor(
                s_var, s_mean, s_mean, ALU.mult),
                reads=[B_stat], writes=[B_stat], tag="ln_v2", small=True)
            P.emit("vector", lambda v: nc.vector.scalar_tensor_tensor(
                s_var, s_sq, 1.0 / D, s_var, ALU.mult, ALU.subtract),
                reads=[B_stat], writes=[B_stat], tag="ln_v3", small=True)
            P.emit("scalar", lambda s: nc.scalar.activation(
                s_var, s_var, AF.Sqrt, bias=s_eps),
                reads=[B_stat], writes=[B_stat], tag="ln_sqrt", small=True)
            P.emit("vector", lambda v: nc.vector.reciprocal(s_rstd, s_var),
                   reads=[B_stat], writes=[B_stat], tag="ln_r1", small=True)
            P.emit("vector", lambda v: nc.vector.scalar_tensor_tensor(
                s_nmr, s_mean, -1.0, s_rstd, ALU.mult, ALU.mult),
                reads=[B_stat], writes=[B_stat], tag="ln_r2", small=True)
            P.emit("vector", lambda v: nc.vector.tensor_scalar(
                out_ap, x_ap, s_rstd, s_nmr, ALU.mult, ALU.add),
                reads=list(bufs_x) + [B_stat], writes=list(bufs_out), tag="ln_n")

        def tm2fm(src_ap, bufs_src, dst_ap, bufs_dst, tag):
            def tmm(tn):
                last = None
                for p2 in range(NPT):
                    bank = 6 + (p2 // 4)
                    last = nc.tensor.transpose(
                        psum[bank][:, (p2 % 4) * 128:(p2 % 4) * 128 + 128],
                        src_ap[:, p2 * 128:(p2 + 1) * 128], ident[:, :])
                return last
            P.emit("tensor", tmm, reads=list(bufs_src) + [B_const],
                   writes=[B_ps[6], B_ps[7]], tag=tag + "mm")
            P.emit("scalar", lambda s: nc.scalar.copy(dst_ap[:, 0:512], psum[6][:, :]),
                   reads=[B_ps[6]], writes=list(bufs_dst), tag=tag + "e0")
            P.emit("vector", lambda v: nc.vector.tensor_copy(
                dst_ap[:, 512:1024], psum[7][:, :]),
                reads=[B_ps[7]], writes=list(bufs_dst), tag=tag + "e1")

        def split_half(src_ap, bufs_src, half, tag):
            hi_ap = st_hi[:, half * D:(half + 1) * D]
            lo_ap = st_lo[:, half * D:(half + 1) * D]
            P.emit("scalar", lambda s, hi_ap=hi_ap: nc.scalar.copy(
                hi_ap, src_ap),
                reads=list(bufs_src), writes=[B_st[half]], tag=tag + "h")
            P.emit("vector", lambda v, hi_ap=hi_ap, lo_ap=lo_ap:
                   nc.vector.tensor_tensor(lo_ap, src_ap, hi_ap, ALU.subtract),
                   reads=list(bufs_src) + [B_st[half]], writes=[B_st[half]],
                   tag=tag + "l")

        wslot_state = {"n": 0}

        def next_wslot(use3=False):
            s = wslot_state["n"]
            wslot_state["n"] = (s + 1) % 2
            return s

        pbank_state = {"n": 0}

        def next_pbank():
            s = pbank_state["n"]
            pbank_state["n"] = (s + 1) % 4
            return s

        HKC = NPT // 2 * 2 * PW

        def load_wpiece(slot, name, pi_):
            P.emit("sync", lambda s, name=name, pi_=pi_, slot=slot:
                   s.dma_start(
                       out=wbuf[slot][:, 0:HKC],
                       in_=whl[name][pi_, :, 0:HKC]),
                   reads=[B_wsc], writes=[B_w[slot][0]], tag="wload")
            P.emit("gpsimd", lambda g, name=name, pi_=pi_, slot=slot:
                   g.dma_start(
                       out=wbuf[slot][:, HKC:2 * HKC],
                       in_=whl[name][pi_, :, HKC:2 * HKC]),
                   reads=[B_wsc], writes=[B_w[slot][1]], tag="wload_g",
                   dma=True)

        def proj_piece(wslot, pbank, half, bias_ap, bias_buf=None):
            def mm(tn, pbank=pbank, bias_ap=bias_ap, wslot=wslot, half=half,
                   hh=0):
                last = None
                for dc in range(hh * 4, hh * 4 + 4):
                    sh = st_hi[:, half * D + dc * 128:half * D + (dc + 1) * 128]
                    sl = st_lo[:, half * D + dc * 128:half * D + (dc + 1) * 128]
                    whi_ap = wbuf[wslot][:, dc * 2 * PW:dc * 2 * PW + PW]
                    wlo_ap = wbuf[wslot][:, dc * 2 * PW + PW:(dc + 1) * 2 * PW]
                    last = nc.tensor.matmul(
                        psum[pbank][:, 0:PW], sh, whi_ap,
                        start=(dc == 0), stop=False)
                    last = nc.tensor.matmul(
                        psum[pbank][:, 0:PW], sh, wlo_ap,
                        start=False, stop=False)
                    last = nc.tensor.matmul(
                        psum[pbank][:, 0:PW], sl, whi_ap,
                        start=False,
                        stop=(hh == 1 and dc == NPT - 1 and bias_ap is None))
                if hh == 1 and bias_ap is not None:
                    last = nc.tensor.matmul(
                        psum[pbank][:, 0:PW], ones2[:, :], bias_ap,
                        start=False, stop=True)
                return last
            extra = [bias_buf] if bias_buf is not None else []
            def mmall(tn, mm=mm):
                mm(tn, hh=0)
                return mm(tn, hh=1)
            P.emit("tensor", mmall,
                   reads=[B_st[half], B_w[wslot][0], B_w[wslot][1],
                          B_const] + extra,
                   writes=[B_ps[pbank]], tag="proj")

        def stream(wname, dst_cb, bias_seg, tag, halves=(0, 1), use3=False):
            for pi_, c0 in enumerate(range(0, D, PW)):
                slot = next_wslot(use3)
                load_wpiece(slot, wname, pi_)
                bias_ap = None
                bias_buf = None
                if bias_seg is not None:
                    bsl = pi_ % 2
                    bias_ap = bias_stage[:, bsl * PW:(bsl + 1) * PW]
                    bias_buf = B_bias[bsl]
                    boff = BSEG[bias_seg] * D + c0
                    P.emit("gpsimd", lambda g, boff=boff, ba=bias_ap:
                           g.dma_start(out=ba, in_=bias_hl[:, boff:boff + PW]),
                           writes=[bias_buf], reads=[B_wsc], tag="bias",
                           dma=True)
                for half in halves:
                    pb_ = next_pbank()
                    proj_piece(slot, pb_, half, bias_ap, bias_buf=bias_buf)
                    dst_cb(c0, pb_, half)

        def emit_mix(kind, half, carry_ap, B_carry, dst_ap, bufs_dst):
            mu_sl = mu[:, kind * NPT:(kind + 1) * NPT]
            omu_sl = omu[:, kind * NPT:(kind + 1) * NPT]
            x3 = xlh(half).rearrange("p (a c) -> p a c", a=NPT)
            o3 = dst_ap.rearrange("p (a c) -> p a c", a=NPT)
            if half == 0:
                carry_src = [carry_ap[:, a2:a2 + 1] for a2 in range(NPT)]
                carry_reads = [B_carry]
            else:
                x3p = xlh(0).rearrange("p (a c) -> p a c", a=NPT)
                carry_src = [x3p[:, a2, 127:128] for a2 in range(NPT)]
                carry_reads = [B_xlnT[0]]

            def mix1(v):
                last = None
                for a2 in range(NPT):
                    last = nc.vector.tensor_scalar(
                        o3[:, a2, :], x3[:, a2, :], mu_sl[:, a2:a2 + 1], None,
                        ALU.mult)
                return last
            P.emit("vector", mix1, reads=[B_xlnT[half], B_const],
                   writes=list(bufs_dst), tag="mix1")

            def mix2(v):
                last = None
                for a2 in range(NPT):
                    last = nc.vector.scalar_tensor_tensor(
                        o3[:, a2, 1:128], x3[:, a2, 0:127], omu_sl[:, a2:a2 + 1],
                        o3[:, a2, 1:128], ALU.mult, ALU.add)
                    last = nc.vector.scalar_tensor_tensor(
                        o3[:, a2, 0:1], carry_src[a2], omu_sl[:, a2:a2 + 1],
                        o3[:, a2, 0:1], ALU.mult, ALU.add)
                return last
            P.emit("vector", mix2,
                   reads=[B_xlnT[half], B_const] + carry_reads,
                   writes=list(bufs_dst), tag="mix2")

        def ffn(ts, carry_ap, B_carry, tag):
            mixdst = (prod, q_fm)
            mixbuf = (B_prod, B_qfm)
            for i in range(2):
                ln_tile(xh(i), [B_x[i]], facc[:, :], [B_facc[0], B_facc[1]])
                tm2fm(facc[:, :], [B_facc[0], B_facc[1]], xlh(i), [B_xlnT[i]], f"{tag}lnT{i}")
            for i in range(2):
                emit_mix(0, i, carry_ap, B_carry, mixdst[i][:, :], [mixbuf[i]])
                split_half(mixdst[i][:, :], [mixbuf[i]], i, f"{tag}sxr{i}")

            def rsig_cb(c0, pb_, half):
                P.emit("scalar", lambda s, c0=c0, pb_=pb_, half=half:
                       nc.scalar.activation(
                           a_ch[:, half * D + c0:half * D + c0 + PW],
                           psum[pb_][:, 0:PW], AF.Sigmoid),
                       reads=[B_ps[pb_]], writes=[B_a[half]], tag=f"{tag}rsig")
            stream("r", rsig_cb, None, f"{tag}Wr", use3=True)
            for i in range(2):
                emit_mix(1, i, carry_ap, B_carry, mixdst[i][:, :], [mixbuf[i]])
                split_half(mixdst[i][:, :], [mixbuf[i]], i, f"{tag}sxk{i}")
            P.emit("vector", lambda v: nc.vector.tensor_copy(
                carry_ap[:, :],
                xlh(1).rearrange("p (a c) -> p a c", a=NPT)[:, :, 127]),
                reads=[B_xlnT[1]], writes=[B_carry], tag=f"{tag}carry", small=True)

            relu_t = (prod, facc)
            relu_b = ([B_prod], [B_facc[0], B_facc[1]])

            def krelu_cb(c0, pb_, half):
                P.emit("vector", lambda v, c0=c0, pb_=pb_, half=half:
                       nc.vector.tensor_scalar_max(
                           relu_t[half][:, c0:c0 + PW], psum[pb_][:, 0:PW], 0.0),
                       reads=[B_ps[pb_]], writes=relu_b[half],
                       tag=f"{tag}krelu")
            stream("k", krelu_cb, None, f"{tag}Wk", use3=True)

            for i in range(2):
                def ktr(tn, i=i):
                    last = None
                    for p2 in range(NPT):
                        bank = 6 + (p2 // 4)
                        last = nc.tensor.transpose(
                            psum[bank][:, (p2 % 4) * 128:(p2 % 4) * 128 + 128],
                            relu_t[i][:, p2 * 128:(p2 + 1) * 128], ident[:, :])
                    return last
                P.emit("tensor", ktr, reads=relu_b[i] + [B_const],
                       writes=[B_ps[6], B_ps[7]], tag=f"{tag}ktr{i}")
                P.emit("scalar", lambda s, i=i: nc.scalar.activation(
                    b_ch[:, i * D:i * D + 512], psum[6][:, :], AF.Square),
                    reads=[B_ps[6]], writes=[B_b[i]], tag=f"{tag}ksq0")
                P.emit("scalar", lambda s, i=i: nc.scalar.activation(
                    b_ch[:, i * D + 512:i * D + 1024], psum[7][:, :], AF.Square),
                    reads=[B_ps[7]], writes=[B_b[i]], tag=f"{tag}ksq1")
                split_half(bh(i), [B_b[i]], i, f"{tag}skt{i}")

            def v_cb(c0, pb_, half):
                vt = relu_t[half]
                vb = relu_b[half]
                P.emit("vector",
                       lambda v, c0=c0, pb_=pb_, half=half, vt=vt:
                       nc.vector.tensor_tensor(
                           vt[:, c0:c0 + PW],
                           a_ch[:, half * D + c0:half * D + c0 + PW],
                           psum[pb_][:, 0:PW], ALU.mult),
                       reads=[B_ps[pb_], B_a[half]], writes=vb, tag=f"{tag}vm")
                P.emit("vector",
                       lambda v, c0=c0, half=half, vt=vt:
                       nc.vector.scalar_tensor_tensor(
                           x_ch[:, half * D + c0:half * D + c0 + PW],
                           vt[:, c0:c0 + PW], 0.5,
                           x_ch[:, half * D + c0:half * D + c0 + PW],
                           ALU.mult, ALU.add),
                       reads=vb + [B_x[half]], writes=[B_x[half]], tag=f"{tag}va")
            stream("v", v_cb, None, f"{tag}Wv", use3=True)
            if ts[0] == 0 and tag[0] in "AC":
                dump(tag[0] + "_xmid", xh(0), D, [B_x[0]])

        def attn_core(t, half):
            qk3 = qk_cmp[:, :].rearrange("p (h o) -> p h o", h=H)
            for ci, m in enumerate(FAR_M):
                rk = NRANK + ci
                if t < m:
                    P.emit("vector", lambda v, rk=rk, qk3=qk3: nc.vector.memset(
                        qk3[:, :, rk:rk + 1], 0.0),
                        writes=[B_qkf], tag=f"fneg{t}_{ci}", small=True)
                    continue
                sl = (t - m) % RING
                P.emit("vector", lambda v, sl=sl, half=half: nc.vector.tensor_tensor(
                    prod[:, :], ah(half),
                    k_tm[:, sl * D:(sl + 1) * D], ALU.mult),
                    reads=[B_a[half], B_ktm[sl]], writes=[B_prod],
                    tag=f"fqk{t}_{ci}")
                P.emit("vector", lambda v, rk=rk, qk3=qk3: nc.vector.tensor_reduce(
                    qk3[:, :, rk:rk + 1],
                    prod[:, :].rearrange("p (h d) -> p h d", h=H), AX.X, ALU.add),
                    reads=[B_prod], writes=[B_qkf], tag=f"frd{t}_{ci}", small=True)
            tm2fm(ah(half), [B_a[half]], q_fm[:, :], [B_qfm], f"qT{t}")
            if t == 0:
                dump("B_qfm", q_fm[:, :], D, [B_qfm])
            ksl = t % RING
            kfsl = t % KFR
            tm2fm(k_tm[:, ksl * D:(ksl + 1) * D], [B_ktm[ksl]],
                  k_fm[:, kfsl * D:(kfsl + 1) * D], [B_kfm[kfsl]], f"kT{t}")
            cstart = {0: 192, 1: 64}.get(t, 0)
            for h in range(H):
                hp = h // 2
                base = 64 * (h % 2)
                bslot = h % 2
                bank = 2 + (h % 2)

                def bmm(tn, h=h, hp=hp, base=base, bank=bank, t=t):
                    last = None
                    segs = []
                    if t >= 2:
                        segs.append(((t - 2) % KFR, 64, 128, 0))
                    if t >= 1:
                        segs.append(((t - 1) % KFR, 0, 128, 64))
                    segs.append((t % KFR, 0, 128, 192))
                    for (sl, a0, a1, oc) in segs:
                        last = nc.tensor.matmul(
                            psum[bank][:, oc:oc + (a1 - a0)],
                            q_fm[base:base + 64, hp * 128:(hp + 1) * 128],
                            k_fm[base:base + 64,
                                 sl * D + hp * 128 + a0:sl * D + hp * 128 + a1],
                            start=True, stop=True)
                    last = nc.tensor.matmul(
                        psum[bank][:, WIN:WIN + NOFF],
                        q_fm[base:base + 64, hp * 128:(hp + 1) * 128],
                        sedup[base:base + 64, :], start=True, stop=True)
                    return last
                kf_reads = [B_kfm[(t - k) % KFR] for k in range(min(3, t + 1))]
                P.emit("tensor", bmm, reads=[B_qfm, B_const] + kf_reads,
                       writes=[B_ps[bank]], tag=f"band{t}_{h}")
                if t < 2:
                    P.emit("vector", lambda v, bslot=bslot: nc.vector.memset(
                        band_scr[:, bslot * WIN:bslot * WIN + WIN], 0.0),
                        writes=[B_band[bslot]], tag="bz")

                P.emit("scalar", lambda o, bank=bank, bslot=bslot,
                       cstart=cstart: nc.scalar.copy(
                           band_scr[:, bslot * WIN + cstart:bslot * WIN + WIN],
                           psum[bank][:, cstart:WIN]),
                       reads=[B_ps[bank]], writes=[B_band[bslot]],
                       tag=f"bev{t}_{h}")
                P.emit("scalar", lambda o, bank=bank, h=h: nc.scalar.copy(
                    qs_cmp[:, h * NOFF:(h + 1) * NOFF],
                    psum[bank][:, WIN:WIN + NOFF]),
                    reads=[B_ps[bank]], writes=[B_qs[h // 8]],
                    tag=f"qsev{t}_{h}")
                P.emit("gpsimd", lambda g, h=h, bslot=bslot: g.local_scatter(
                    qk_cmp[:, h * NOFF:h * NOFF + NRANK].bitcast(I16),
                    band_scr[:, bslot * WIN:bslot * WIN + WIN].bitcast(I16),
                    idxe[:, :], 128, 2 * NRANK, 2 * WIN),
                    reads=[B_band[bslot], B_const], writes=[B_qk[h // 8]],
                    tag=f"ext{t}_{h}")
            if t == 0:
                dump("B_qk", qk_cmp[:, :], H * NOFF, [B_qk[0], B_qk[1], B_qkf])
                dump("B_qs", qs_cmp[:, :], H * NOFF, [B_qs[0], B_qs[1]])
            pbT = (pbandT[:, :], q_fm[:, 0:PWIN])
            pbT_buf = (B_pbandT, B_qfm)

            def emit_pvmm(h):
                def pvmm(tn, h=h, t=t):
                    bank = 4 + (h // 8)
                    ccol = (h % 8) * 64
                    psl = h % 2
                    last = None
                    nch = 0
                    bank_first = (h == 0 or h == 8)
                    bank_last = (h == 7 or h == 15)
                    nvalid = sum(1 for c_ in range(3) if t - 2 + c_ >= 0)
                    for ch_ in range(3):
                        sl_t = t - 2 + ch_
                        if sl_t < 0:
                            continue
                        sl = sl_t % RING
                        last = nc.tensor.matmul(
                            psum[bank][:, ccol:ccol + 64],
                            pbT[psl][:, ch_ * 128:(ch_ + 1) * 128],
                            v_tm[:, sl * D + h * HD:sl * D + (h + 1) * HD],
                            start=(bank_first and nch == 0),
                            stop=(bank_last and nch == nvalid - 1),
                            skip_group_check=True)
                        nch += 1
                    return last
                vt_reads = [B_vtm[(t - 2 + c_) % RING] for c_ in range(3)
                            if t - 2 + c_ >= 0]
                P.emit("tensor", pvmm, reads=[pbT_buf[h % 2]] + vt_reads,
                       writes=[B_ps[4 + (h // 8)]], tag=f"pv{t}_{h}")

            pend = None
            for grp in range(2):
                hg0 = grp * 8
                cs = hg0 * NOFF
                ce = cs + 8 * NOFF
                pg = p_cmp[:, cs:ce]
                lgs = pg.rearrange("p (h o) -> p h o", h=8)
                P.emit("vector", lambda v, pg=pg, cs=cs, ce=ce:
                       nc.vector.tensor_tensor(
                           pg, qk_cmp[:, cs:ce], qs_cmp[:, cs:ce], ALU.mult),
                       reads=[B_qk[grp], B_qkf, B_qs[grp]], writes=[B_p[grp]],
                       tag=f"lg1_{grp}")
                P.emit("vector", lambda v, pg=pg, cs=cs, ce=ce:
                       nc.vector.tensor_tensor(
                           pg, pg, qk_cmp[:, cs:ce], ALU.add),
                       reads=[B_p[grp], B_qk[grp], B_qkf], writes=[B_p[grp]],
                       tag=f"lg2_{grp}")
                P.emit("vector", lambda v, pg=pg, cs=cs, ce=ce:
                       nc.vector.scalar_tensor_tensor(
                           pg, pg, INV, pbc[:, cs:ce], ALU.mult, ALU.add),
                       reads=[B_p[grp], B_const], writes=[B_p[grp]],
                       tag=f"lg3_{grp}")
                for ci, m in enumerate(FAR_M):
                    if t < m:
                        rk = NRANK + ci
                        P.emit("vector", lambda v, rk=rk, lgs=lgs:
                               nc.vector.memset(lgs[:, :, rk:rk + 1], NEG),
                               writes=[B_p[grp]], tag=f"lgneg{t}_{ci}_{grp}",
                               small=True)
                if t == 0:
                    P.emit("vector", lambda v, lgs=lgs: nc.vector.tensor_tensor(
                        lgs[:, :, 0:NRANK], lgs[:, :, 0:NRANK],
                        mask0[:, :].unsqueeze(1).broadcast_to([128, 8, NRANK]),
                        ALU.add),
                        reads=[B_p[grp], B_const], writes=[B_p[grp]],
                        tag=f"lgmask_{grp}")
                elif t == 1:
                    P.emit("vector", lambda v, pg=pg: nc.vector.memset(
                        pg[0:64, :].rearrange(
                            "p (h o) -> p h o", h=8)[:, :, NRANK - 1:NRANK], NEG),
                        reads=[B_p[grp]], writes=[B_p[grp]],
                        tag=f"lgmask1_{grp}", small=True)
                if t == 0 and grp == 1:
                    dump("B_lg", p_cmp[:, :], H * NOFF, [B_p[0], B_p[1]])
                mx = red16[:, hg0:hg0 + 8]
                sm = red16[:, H + hg0:H + hg0 + 8]
                rc = rec16[:, hg0:hg0 + 8]
                P.emit("vector", lambda v, lgs=lgs, mx=mx: nc.vector.tensor_reduce(
                    mx.rearrange("p (h o) -> p h o", o=1), lgs, AX.X, ALU.max),
                    reads=[B_p[grp]], writes=[B_red[grp]], tag=f"smax{grp}",
                    small=True)
                P.emit("vector", lambda v, lgs=lgs, mx=mx: nc.vector.tensor_tensor(
                    lgs, lgs,
                    mx.rearrange("p (h o) -> p h o", o=1).broadcast_to(
                        [128, 8, NOFF]), ALU.subtract),
                    reads=[B_p[grp], B_red[grp]], writes=[B_p[grp]],
                    tag=f"ssub{grp}")
                P.emit("scalar", lambda s, pg=pg: nc.scalar.activation(
                    pg, pg, AF.Exp),
                    reads=[B_p[grp]], writes=[B_p[grp]], tag=f"sexp{grp}")
                P.emit("vector", lambda v, lgs=lgs, sm=sm: nc.vector.tensor_reduce(
                    sm.rearrange("p (h o) -> p h o", o=1), lgs, AX.X, ALU.add),
                    reads=[B_p[grp]], writes=[B_red[grp]], tag=f"ssum{grp}",
                    small=True)
                P.emit("vector", lambda v, rc=rc, sm=sm: nc.vector.reciprocal(
                    rc, sm),
                    reads=[B_red[grp]], writes=[B_rec[grp]], tag=f"srcp1{grp}",
                    small=True)
                P.emit("vector", lambda v, rc=rc, hg0=hg0: nc.vector.tensor_tensor(
                    rc, rc, gain16[:, hg0:hg0 + 8], ALU.mult),
                    reads=[B_rec[grp], B_const], writes=[B_rec[grp]],
                    tag=f"srcp2{grp}", small=True)
                P.emit("vector", lambda v, lgs=lgs, rc=rc: nc.vector.tensor_tensor(
                    lgs, lgs,
                    rc.rearrange("p (h o) -> p h o", o=1).broadcast_to(
                        [128, 8, NOFF]), ALU.mult),
                    reads=[B_p[grp], B_rec[grp]], writes=[B_p[grp]],
                    tag=f"smul{grp}")
                if t == 0 and grp == 1:
                    dump("B_p", p_cmp[:, :], H * NOFF, [B_p[0], B_p[1]])
                first = True
                facc_g = facc[:, grp * 512:(grp + 1) * 512]
                prod_g = prod[:, grp * 512:(grp + 1) * 512]
                for ci, m in enumerate(FAR_M):
                    rk = NRANK + ci
                    if t < m:
                        continue
                    sl = (t - m) % RING
                    pb_ap = lgs[:, :, rk:rk + 1].broadcast_to([128, 8, HD])
                    v_g = v_tm[:, sl * D + grp * 512:sl * D + (grp + 1) * 512]
                    if first:
                        P.emit("vector", lambda v, pb_ap=pb_ap, v_g=v_g,
                               facc_g=facc_g: nc.vector.tensor_tensor(
                                   facc_g.rearrange("p (h d) -> p h d", h=8),
                                   v_g.rearrange("p (h d) -> p h d", h=8),
                                   pb_ap, ALU.mult),
                               reads=[B_p[grp], B_vtm[sl]],
                               writes=[B_facc[grp]], tag=f"fpv{t}_{ci}_{grp}")
                        first = False
                    else:
                        P.emit("vector", lambda v, pb_ap=pb_ap, v_g=v_g,
                               prod_g=prod_g: nc.vector.tensor_tensor(
                                   prod_g.rearrange("p (h d) -> p h d", h=8),
                                   v_g.rearrange("p (h d) -> p h d", h=8),
                                   pb_ap, ALU.mult),
                               reads=[B_p[grp], B_vtm[sl]], writes=[B_prod],
                               tag=f"fpm{t}_{ci}_{grp}")
                        P.emit("vector", lambda v, facc_g=facc_g, prod_g=prod_g:
                               nc.vector.tensor_tensor(
                                   facc_g, facc_g, prod_g, ALU.add),
                               reads=[B_prod, B_facc[grp]],
                               writes=[B_facc[grp]], tag=f"fpa{t}_{ci}_{grp}")
                if first:
                    P.emit("vector", lambda v, facc_g=facc_g: nc.vector.memset(
                        facc_g, 0.0),
                        writes=[B_facc[grp]], tag=f"fz{grp}")
                for h in range(hg0, hg0 + 8):
                    psl = h % 2
                    tbank = 6 + psl
                    P.emit("gpsimd", lambda g, h=h, psl=psl: g.local_scatter(
                        pband[psl][:, :].bitcast(I16),
                        p_cmp[:, h * NOFF:h * NOFF + NRANK].bitcast(I16),
                        idxb[:, :], 128, 2 * PWIN, 2 * NRANK),
                        reads=[B_p[grp], B_const], writes=[B_pband[psl]],
                        tag=f"bld{t}_{h}")

                    def ptr(tn, psl=psl, tbank=tbank):
                        last = None
                        for ch_ in range(3):
                            last = nc.tensor.transpose(
                                psum[tbank][:, ch_ * 128:(ch_ + 1) * 128],
                                pband[psl][:, ch_ * 128:(ch_ + 1) * 128],
                                ident[:, :])
                        return last
                    P.emit("tensor", ptr, reads=[B_pband[psl], B_const],
                           writes=[B_ps[tbank]], tag=f"ptr{t}_{h}")
                    if pend is not None:
                        emit_pvmm(pend)
                    P.emit("scalar", lambda o, psl=psl, tbank=tbank:
                           nc.scalar.copy(
                               pbT[psl][:, :], psum[tbank][:, 0:PWIN]),
                           reads=[B_ps[tbank]], writes=[pbT_buf[psl]],
                           tag=f"pte{t}_{h}")
                    pend = h
            emit_pvmm(pend)
            for hf in range(2):
                P.emit("vector", lambda v, hf=hf: nc.vector.scalar_tensor_tensor(
                    prod[:, hf * 512:(hf + 1) * 512],
                    psum[4 + hf][:, :], 1.0,
                    facc[:, hf * 512:(hf + 1) * 512], ALU.mult, ALU.add),
                    reads=[B_ps[4 + hf], B_facc[hf]], writes=[B_prod],
                    tag=f"goadd{t}_{hf}")
            P.emit("vector", lambda v, half=half: nc.vector.tensor_tensor(
                prod[:, :], prod[:, :], bh(half), ALU.mult),
                reads=[B_prod, B_b[half]], writes=[B_prod], tag=f"gomul{t}")
            tm2fm(prod[:, :], [B_prod], xlh(half), [B_xlnT[half]], f"goT{t}")
            if t == 0:
                dump("B_go", xlh(0), D, [B_xlnT[0]])

        def attn(ts):
            for i in range(2):
                ln_tile(xh(i), [B_x[i]], facc[:, :], [B_facc[0], B_facc[1]])
                tm2fm(facc[:, :], [B_facc[0], B_facc[1]], xlh(i), [B_xlnT[i]], f"b{ts[i]}lnT")
                split_half(xlh(i), [B_xlnT[i]], i, f"b{ts[i]}sln")

            def qev_cb(c0, pb_, half):
                P.emit("scalar", lambda s, c0=c0, pb_=pb_, half=half:
                       nc.scalar.copy(
                           a_ch[:, half * D + c0:half * D + c0 + PW],
                           psum[pb_][:, 0:PW]),
                       reads=[B_ps[pb_]], writes=[B_a[half]], tag="qev")
            stream("q", qev_cb, "q", "Wq")

            def gev_cb(c0, pb_, half):
                P.emit("scalar", lambda s, c0=c0, pb_=pb_, half=half:
                       nc.scalar.activation(
                           b_ch[:, half * D + c0:half * D + c0 + PW],
                           psum[pb_][:, 0:PW], AF.Sigmoid),
                       reads=[B_ps[pb_]], writes=[B_b[half]], tag="gev")
            stream("g", gev_cb, "g", "Wg")

            for i, t in enumerate(ts):
                ksl = t % RING

                def kev_cb(c0, pb_, half, ksl=ksl):
                    P.emit("scalar", lambda s, c0=c0, pb_=pb_, ksl=ksl:
                           nc.scalar.copy(
                               k_tm[:, ksl * D + c0:ksl * D + c0 + PW],
                               psum[pb_][:, 0:PW]),
                           reads=[B_ps[pb_]], writes=[B_ktm[ksl]], tag="kev")
                stream("k2", kev_cb, "k2", f"Wk2_{t}", halves=(i,))

                def vev_cb(c0, pb_, half, ksl=ksl):
                    P.emit("scalar", lambda s, c0=c0, pb_=pb_, ksl=ksl:
                           nc.scalar.copy(
                               v_tm[:, ksl * D + c0:ksl * D + c0 + PW],
                               psum[pb_][:, 0:PW]),
                           reads=[B_ps[pb_]], writes=[B_vtm[ksl]], tag="vev")
                stream("v2", vev_cb, "v2", f"Wv2_{t}", halves=(i,))
                if t == 0:
                    dump("B_q", ah(0), D, [B_a[0]])
                    dump("B_k", k_tm[:, 0:D], D, [B_ktm[0]])
                    dump("B_v", v_tm[:, 0:D], D, [B_vtm[0]])
                    dump("B_gate", bh(0), D, [B_b[0]])
                attn_core(t, i)

            for i in range(2):
                split_half(xlh(i), [B_xlnT[i]], i, f"b{ts[i]}sgo")

            def oacc_cb(c0, pb_, half):
                x_sl = x_ch[:, half * D + c0:half * D + c0 + PW]
                P.emit("vector", lambda v, pb_=pb_, x_sl=x_sl, half=half:
                       nc.vector.tensor_tensor(
                           x_sl, x_sl, psum[pb_][:, 0:PW], ALU.add),
                       reads=[B_ps[pb_], B_x[half]], writes=[B_x[half]],
                       tag="oacc")
            stream("o", oacc_cb, "o", "Wo")
            if ts[0] == 0:
                dump("B_x", xh(0), D, [B_x[0]])

        assert n_tiles % 2 == 0
        for c in range(n_tiles // 2):
            ts = (2 * c, 2 * c + 1)
            for i, t in enumerate(ts):
                P.emit("sync", lambda s, t=t, i=i: s.dma_start(
                    out=xh(i), in_=x_in[t * 128:(t + 1) * 128, :]),
                    writes=[B_x[i]], tag=f"xload{t}")
            ffn(ts, carryA[:, :], B_carryA, f"A{c}_")
            attn(ts)
            ffn(ts, carryC[:, :], B_carryC, f"C{c}_")
            for i, t in enumerate(ts):
                P.emit("sync", lambda s, t=t, i=i: s.dma_start(
                    out=y_out[t * 128:(t + 1) * 128, :], in_=xh(i)),
                    reads=[B_x[i]], writes=[B_out[t]], tag=f"store{t}")

        P.emit("tensor", lambda tn: nc.tensor.nop(),
               reads=[B_out[t] for t in range(n_tiles)], tag="fence")

        runner = P.make_runner(sems)
        with nc.Block() as block:
            block.sync(lambda o: runner(o, "sync"))
            block.tensor(lambda o: runner(o, "tensor"))
            block.vector(lambda o: runner(o, "vector"))
            block.scalar(lambda o: runner(o, "scalar"))
            block.gpsimd(lambda o: runner(o, "gpsimd"))

    lower_extended_insts(nc)
    return nc


def _prep_inputs(inputs):
    consts = _host_consts()
    pos_bias = np.asarray(inputs["L0_attn_pos_bias"], np.float32)
    scale_embed = np.asarray(inputs["L0_attn_scale_embed"], np.float32)
    if_gain = np.asarray(inputs["L0_attn_if_gain"], np.float32)
    mu_r = np.asarray(inputs["L0_ffn_mu_r"], np.float32)
    mu_k = np.asarray(inputs["L0_ffn_mu_k"], np.float32)
    sedup = np.empty((128, NOFF), np.float32)
    sedup[0:64] = scale_embed.T
    sedup[64:128] = scale_embed.T
    pbconst = np.ascontiguousarray(
        np.broadcast_to(pos_bias.T.reshape(1, H * NOFF), (128, H * NOFF)))
    gain16 = np.ascontiguousarray(
        np.broadcast_to(if_gain.reshape(1, H), (128, H)))
    mu_in = np.empty((128, 2 * NPT), np.float32)
    mu_in[:, 0:NPT] = mu_r.reshape(NPT, 128).T
    mu_in[:, NPT:2 * NPT] = mu_k.reshape(NPT, 128).T
    common = dict(
        qkv_wT=np.ascontiguousarray(np.asarray(inputs["L0_attn_qkv_w"], np.float32).T),
        gate_wT=np.ascontiguousarray(np.asarray(inputs["L0_attn_gate_w"], np.float32).T),
        out_wT=np.ascontiguousarray(np.asarray(inputs["L0_attn_out_w"], np.float32).T),
        WrT=np.ascontiguousarray(np.asarray(inputs["L0_ffn_W_r"], np.float32).T),
        WkT=np.ascontiguousarray(np.asarray(inputs["L0_ffn_W_k"], np.float32).T),
        WvT=np.ascontiguousarray(np.asarray(inputs["L0_ffn_W_v"], np.float32).T),
        qkv_b=np.asarray(inputs["L0_attn_qkv_b"], np.float32).reshape(1, -1),
        gate_b=np.asarray(inputs["L0_attn_gate_b"], np.float32).reshape(1, -1),
        out_b=np.asarray(inputs["L0_attn_out_b"], np.float32).reshape(1, -1),
        sedup=sedup, pbconst=pbconst, gain16=gain16, mu_in=mu_in,
        identity=consts["identity"],
        idx_ext=consts["idx_ext"], idx_bld=consts["idx_bld"],
        mask0=consts["mask0"],
    )
    return common


def kernel(**inputs):
    x = np.ascontiguousarray(np.asarray(inputs["L0_attn_x"], np.float32))
    B = x.shape[0]
    common = _prep_inputs(inputs)
    nc = build_nc(n_tiles=32)
    in_maps = [dict(common, x=np.ascontiguousarray(x[b])) for b in range(B)]
    res = run_bass_kernel_spmd(nc, in_maps, core_ids=list(range(B)))
    return np.stack([res.results[b]["y"] for b in range(B)], axis=0)


# version 08
# speedup vs baseline: 1.0884x; 1.0265x over previous
import numpy as np
from contextlib import ExitStack

import concourse.bass as bass
import concourse.mybir as mybir
from concourse.bass_utils import run_bass_kernel_spmd
from concourse import library_config
from concourse.library_overlay import lower_extended_insts

F32 = mybir.dt.float32
BF16 = mybir.dt.bfloat16
I16 = mybir.dt.int16
AF = mybir.ActivationFunctionType
ALU = mybir.AluOpType
AX = mybir.AxisListType

D = 1024
H = 16
HD = 64
NPT = 8
NEG = -1.0e30
EPS = 1e-5
INV = 0.125

OFFSETS = sorted(set(range(33)) | {48, 64, 96, 128, 192, 256, 384, 512, 768, 1024, 1536})
NEAR = [o for o in OFFSETS if o <= 192]
FAR = [o for o in OFFSETS if o > 192]
FAR_M = [o // 128 for o in FAR]
NRANK = len(NEAR)
NOFF = len(OFFSETS)
RING = 13
KFR = 3
WIN = 320
PWIN = 384
PW = 256
ENGINES = ("tensor", "vector", "scalar", "gpsimd", "sync")


def _host_consts():
    c = {}
    c["identity"] = np.eye(128, dtype=np.float32)
    idx_ext = np.full((128, 2 * WIN), -1, np.int16)
    for n in range(128):
        for r, o in enumerate(NEAR):
            col = n + 192 - o
            if 0 <= col < WIN:
                idx_ext[n, 2 * col] = 2 * r
                idx_ext[n, 2 * col + 1] = 2 * r + 1
    c["idx_ext"] = idx_ext
    idx_bld = np.full((128, 2 * NRANK), -1, np.int16)
    for n in range(128):
        for r, o in enumerate(NEAR):
            w = n - o + 256
            idx_bld[n, 2 * r] = 2 * w
            idx_bld[n, 2 * r + 1] = 2 * w + 1
    c["idx_bld"] = idx_bld
    for j in (0, 1):
        m = np.zeros((128, NRANK), np.float32)
        for n in range(128):
            for r, o in enumerate(NEAR):
                if 128 * j + n < o:
                    m[n, r] = NEG
        c[f"mask{j}"] = m
    return c


class Buf:
    __slots__ = ("last_w", "readers", "name")

    def __init__(self, name=""):
        self.last_w = None
        self.readers = []
        self.name = name


class Prog:
    def __init__(self, serial=False, same_engine_barriers=True):
        self.ops = {e: [] for e in ENGINES}
        self.serial = serial
        self.seb = same_engine_barriers
        self.last_tok = None

    def emit(self, eng, fn, reads=(), writes=(), tag="", small=False, dma=False):
        is_dma = dma or eng == "sync"
        deps = []
        if self.serial and self.last_tok is not None:
            deps.append(self.last_tok)
        for b in reads:
            if b.last_w is not None:
                deps.append(b.last_w)
        for b in writes:
            deps.extend(b.readers)
            if b.last_w is not None:
                deps.append(b.last_w)
        uniq = {}
        for (pe, pi, pd) in deps:
            if pe == eng and not pd and not self.serial:
                if eng in ("tensor", "gpsimd"):
                    continue
                if (not self.ops[pe][pi].get("small") and not self.seb):
                    continue
            uniq[(pe, pi)] = (pe, pi, pd)
        idx = len(self.ops[eng])
        self.ops[eng].append({"fn": fn, "deps": list(uniq.values()),
                              "inc": is_dma, "tag": tag, "small": small,
                              "step": 16 if is_dma else 1})
        for (pe, pi, _d) in uniq.values():
            self.ops[pe][pi]["inc"] = True
        tok = (eng, idx, is_dma)
        self.last_tok = tok
        for b in reads:
            b.readers.append(tok)
        for b in writes:
            b.last_w = tok
            b.readers = []
        return tok

    def make_runner(self, sems):
        prefix = {}
        for e in ENGINES:
            cnt = 0
            pr = []
            for op in self.ops[e]:
                if op["inc"]:
                    cnt += op["step"]
                pr.append(cnt)
            prefix[e] = pr

        def run_engine(eng_obj, e):
            waited = {}
            for op in self.ops[e]:
                for (pe, pi, _d) in op["deps"]:
                    val = prefix[pe][pi]
                    if waited.get(pe, 0) < val:
                        eng_obj.wait_ge(sems[pe], val)
                        waited[pe] = val
                last = op["fn"](eng_obj)
                if op["inc"]:
                    assert last is not None, f"op returned None: {op['tag']}"
                    last.then_inc(sems[e], op["step"])
        return run_engine


def build_nc(n_tiles=32, dumps=(), serial=False, seb=True):
    nc = bass.Bass("TRN2", target_bir_lowering=False)
    P = Prog(serial=serial, same_engine_barriers=seb)
    dump_regs = {}

    x_in = nc.dram_tensor("x", [4096, D], F32, kind="ExternalInput").ap()
    y_out = nc.dram_tensor("y", [4096, D], F32, kind="ExternalOutput").ap()
    qkv_wT = nc.dram_tensor("qkv_wT", [D, 3 * D], F32, kind="ExternalInput").ap()
    gate_wT = nc.dram_tensor("gate_wT", [D, D], F32, kind="ExternalInput").ap()
    out_wT = nc.dram_tensor("out_wT", [D, D], F32, kind="ExternalInput").ap()
    WrT_d = nc.dram_tensor("WrT", [D, D], F32, kind="ExternalInput").ap()
    WkT_d = nc.dram_tensor("WkT", [D, D], F32, kind="ExternalInput").ap()
    WvT_d = nc.dram_tensor("WvT", [D, D], F32, kind="ExternalInput").ap()
    qkv_b = nc.dram_tensor("qkv_b", [1, 3 * D], F32, kind="ExternalInput").ap()
    gate_b = nc.dram_tensor("gate_b", [1, D], F32, kind="ExternalInput").ap()
    out_b = nc.dram_tensor("out_b", [1, D], F32, kind="ExternalInput").ap()
    sedup_in = nc.dram_tensor("sedup", [128, NOFF], F32, kind="ExternalInput").ap()
    pb_in = nc.dram_tensor("pbconst", [128, H * NOFF], F32, kind="ExternalInput").ap()
    gain_in = nc.dram_tensor("gain16", [128, H], F32, kind="ExternalInput").ap()
    mu_in = nc.dram_tensor("mu_in", [128, 2 * NPT], F32, kind="ExternalInput").ap()
    ident_in = nc.dram_tensor("identity", [128, 128], F32, kind="ExternalInput").ap()
    idxe_in = nc.dram_tensor("idx_ext", [128, 2 * WIN], I16, kind="ExternalInput").ap()
    idxb_in = nc.dram_tensor("idx_bld", [128, 2 * NRANK], I16, kind="ExternalInput").ap()
    mask0_in = nc.dram_tensor("mask0", [128, NRANK], F32, kind="ExternalInput").ap()

    WSRC = {
        "q": qkv_wT[:, 0:D], "k2": qkv_wT[:, D:2 * D], "v2": qkv_wT[:, 2 * D:3 * D],
        "g": gate_wT, "o": out_wT, "r": WrT_d, "k": WkT_d, "v": WvT_d,
    }
    whl = {n: nc.dram_tensor(f"whl_{n}", [4, 128, NPT * 2 * PW], BF16,
                             kind="Internal").ap() for n in WSRC}
    bias_hl = nc.dram_tensor("bias_hl", [2, 5 * D], BF16, kind="Internal").ap()
    BSEG = {"q": 0, "k2": 1, "v2": 2, "g": 3, "o": 4}
    BSRC = {"q": qkv_b[:, 0:D], "k2": qkv_b[:, D:2 * D],
            "v2": qkv_b[:, 2 * D:3 * D], "g": gate_b, "o": out_b}

    with ExitStack() as stack:
        e = stack.enter_context

        def sb(name, shape, dt=F32):
            return e(nc.sbuf_tensor(name, shape, dt))

        k_tm = sb("k_tm", [128, RING * D])
        v_tm = sb("v_tm", [128, RING * D])
        k_fm = sb("k_fm", [128, KFR * D])
        CT = 2
        x_ch = sb("x_ch", [128, CT * D])
        xlnT_ch = sb("xlnT_ch", [128, CT * D])
        a_ch = sb("a_ch", [128, CT * D])
        b_ch = sb("b_ch", [128, CT * D])
        wbuf = [sb(f"wbuf{i}", [128, NPT * 2 * PW], BF16) for i in range(2)]
        st_hi = sb("st_hi", [128, CT * D], BF16)
        st_lo = sb("st_lo", [128, CT * D], BF16)
        q_fm = sb("q_fm", [128, D])
        band_scr = sb("band_scr", [128, 2 * WIN])
        qslab = sb("qslab", [128, 3 * H * NOFF])
        qs_cmp = qslab[:, 0:H * NOFF]
        qk_cmp = qslab[:, H * NOFF:2 * H * NOFF]
        p_cmp = qslab[:, 2 * H * NOFF:3 * H * NOFF]

        prod = sb("prod", [128, D])
        facc = sb("facc", [128, D])
        pband = [sb(f"pband{i}", [128, PWIN]) for i in range(2)]
        pbandT = sb("pbandT", [128, PWIN])
        stat = sb("stat", [128, 8])
        red16 = sb("red16", [128, 2 * H])
        rec16 = sb("rec16", [128, H])
        carryA = sb("carryA", [128, NPT])
        carryC = sb("carryC", [128, NPT])
        sedup = sb("sedup_sb", [128, NOFF])
        pbc = sb("pbc_sb", [128, H * NOFF])
        gain16 = sb("gain16_sb", [128, H])
        mu = sb("mu_sb", [128, 2 * NPT])
        omu = sb("omu_sb", [128, 2 * NPT])
        ident = sb("ident_sb", [128, 128])
        ones2 = sb("ones2_sb", [2, 128], BF16)
        bias_stage = sb("bias_stage", [2, 2 * PW], BF16)
        idxe = sb("idxe_sb", [128, 2 * WIN], I16)
        idxb = sb("idxb_sb", [128, 2 * NRANK], I16)
        mask0 = sb("mask0_sb", [128, NRANK])

        psum = [e(nc.psum_tensor(f"ps{i}", [128, 512], F32)) for i in range(8)]
        sems = {en: e(nc.semaphore(f"sem_{en}")) for en in ENGINES}

        B_ktm = [Buf(f"ktm{s}") for s in range(RING)]
        B_vtm = [Buf(f"vtm{s}") for s in range(RING)]
        B_kfm = [Buf(f"kfm{s}") for s in range(KFR)]
        B_x = [Buf(f"x{i}") for i in range(2)]
        B_xlnT = [Buf(f"xlnT{i}") for i in range(2)]
        B_a = [Buf(f"a{i}") for i in range(2)]
        B_b = [Buf(f"b{i}") for i in range(2)]
        B_w = [[Buf(f"w{i}_{hh}") for hh in range(2)] for i in range(2)]
        B_st = [Buf(f"st{i}") for i in range(2)]
        B_qfm = Buf("qfm")
        B_band = [Buf(f"band{i}") for i in range(2)]
        B_qs = [Buf(f"qs{g}") for g in range(2)]
        B_qk = [Buf(f"qk{g}") for g in range(2)]
        B_qkf = Buf("qkf")
        B_p = [Buf(f"p{g}") for g in range(2)]
        B_prod = Buf("prod")
        B_facc = [Buf(f"facc{g}") for g in range(2)]
        B_pband = [Buf(f"pband{i}") for i in range(2)]
        B_pbandT = Buf("pbandT")
        B_stat = Buf("stat")
        B_red = [Buf(f"red{g}") for g in range(2)]
        B_rec = [Buf(f"rec{g}") for g in range(2)]
        B_carryA = Buf("carryA")
        B_carryC = Buf("carryC")
        B_ps = [Buf(f"ps{i}") for i in range(8)]
        B_const = Buf("const")
        B_bias = [Buf("bias0"), Buf("bias1")]
        B_wsc = Buf("wscratch")
        B_out = [Buf(f"out{t}") for t in range(n_tiles)]

        def xh(i):
            return x_ch[:, i * D:(i + 1) * D]

        def xlh(i):
            return xlnT_ch[:, i * D:(i + 1) * D]

        def ah(i):
            return a_ch[:, i * D:(i + 1) * D]

        def bh(i):
            return b_ch[:, i * D:(i + 1) * D]

        def dump(tag, src_ap, cols, reads):
            if tag not in dumps or tag in dump_regs:
                return
            t_ = nc.dram_tensor("dbg_" + tag, [128, cols], F32,
                                kind="ExternalOutput").ap()
            dump_regs[tag] = True
            P.emit("sync", lambda s, t_=t_, src_ap=src_ap: s.dma_start(
                out=t_, in_=src_ap), reads=list(reads), tag="dump_" + tag)

        for dst, src in ((sedup[:, :], sedup_in), (pbc[:, :], pb_in),
                         (gain16[:, :], gain_in), (mu[:, :], mu_in),
                         (ident[:, :], ident_in),
                         (idxe[:, :], idxe_in),
                         (idxb[:, :], idxb_in), (mask0[:, :], mask0_in)):
            P.emit("sync", lambda s, d=dst, so=src: s.dma_start(out=d, in_=so),
                   writes=[B_const], tag="cload")

        def _loadlib(g):
            g.load_library(library_config.local_scatter)
            return g.nop()
        P.emit("gpsimd", _loadlib, tag="loadlib")
        P.emit("vector", lambda v: nc.vector.tensor_scalar(
            omu[:, :], mu[:, :], -1.0, 1.0, ALU.mult, ALU.add),
            reads=[B_const], writes=[B_const], tag="omu", small=True)
        for t_, bufs in ((k_tm, B_ktm), (v_tm, B_vtm), (k_fm, B_kfm)):
            P.emit("vector", lambda v, tt=t_: nc.vector.memset(tt[:, :], 0.0),
                   writes=list(bufs), tag="zring")
        P.emit("vector", lambda v: nc.vector.memset(carryA[:, :], 0.0),
               writes=[B_carryA], tag="zcA", small=True)
        P.emit("vector", lambda v: nc.vector.memset(stat[:, 6:7], float(EPS)),
               writes=[B_stat], tag="eps", small=True)
        P.emit("vector", lambda v: nc.vector.memset(ones2[:, :], 1.0),
               writes=[B_const], tag="ones2", small=True)
        P.emit("vector", lambda v: nc.vector.memset(carryC[:, :], 0.0),
               writes=[B_carryC], tag="zcC", small=True)

        for name in WSRC:
            src3 = WSRC[name].rearrange("(a p) c -> p a c", p=128)
            for pi_ in range(4):
                c0 = pi_ * PW
                xv = x_ch[:, :].rearrange("p (a c) -> p a c", a=NPT)
                P.emit("sync", lambda s, src3=src3, c0=c0, xv=xv: s.dma_start(
                    out=xv, in_=src3[:, :, c0:c0 + PW]),
                    writes=[B_x[0], B_x[1]], tag="wsp_ld")
                P.emit("scalar", lambda s: nc.scalar.copy(
                    st_hi[:, :], x_ch[:, :]),
                    reads=[B_x[0], B_x[1]], writes=[B_st[0], B_st[1]],
                    tag="wsp_hi")
                P.emit("vector", lambda v: nc.vector.tensor_tensor(
                    st_lo[:, :], x_ch[:, :], st_hi[:, :], ALU.subtract),
                    reads=[B_x[0], B_x[1], B_st[0], B_st[1]],
                    writes=[B_st[0], B_st[1]], tag="wsp_lo")
                w5 = whl[name][pi_, :, :].rearrange(
                    "p (a h c) -> p a h c", a=NPT, h=2)
                P.emit("sync", lambda s, w5=w5: s.dma_start(
                    out=w5[:, :, 0, :],
                    in_=st_hi[:, :].rearrange("p (a c) -> p a c", a=NPT)),
                    reads=[B_st[0], B_st[1]], writes=[B_wsc], tag="wsp_sh")
                P.emit("gpsimd", lambda g, w5=w5: g.dma_start(
                    out=w5[:, :, 1, :],
                    in_=st_lo[:, :].rearrange("p (a c) -> p a c", a=NPT)),
                    reads=[B_st[0], B_st[1]], writes=[B_wsc], tag="wsp_sl",
                    dma=True)

        for bname, bsrc in BSRC.items():
            seg = BSEG[bname]
            P.emit("sync", lambda s, bsrc=bsrc: s.dma_start(
                out=prod[0:1, :], in_=bsrc),
                writes=[B_prod], tag="bsp_ld")
            P.emit("scalar", lambda s: nc.scalar.copy(
                st_hi[0:1, 0:D], prod[0:1, :]),
                reads=[B_prod], writes=[B_st[0]], tag="bsp_hi", small=True)
            P.emit("vector", lambda v: nc.vector.tensor_tensor(
                st_lo[0:1, 0:D], prod[0:1, :], st_hi[0:1, 0:D], ALU.subtract),
                reads=[B_prod, B_st[0]], writes=[B_st[0]], tag="bsp_lo",
                small=True)
            P.emit("sync", lambda s, seg=seg: s.dma_start(
                out=bias_hl[0:1, seg * D:(seg + 1) * D], in_=st_hi[0:1, 0:D]),
                reads=[B_st[0]], writes=[B_wsc], tag="bsp_sh")
            P.emit("sync", lambda s, seg=seg: s.dma_start(
                out=bias_hl[1:2, seg * D:(seg + 1) * D], in_=st_lo[0:1, 0:D]),
                reads=[B_st[0]], writes=[B_wsc], tag="bsp_sl")

        def ln_tile(x_ap, bufs_x, out_ap, bufs_out):
            s_sum = stat[:, 0:1]
            s_sq = stat[:, 1:2]
            s_mean = stat[:, 2:3]
            s_var = stat[:, 3:4]
            s_rstd = stat[:, 4:5]
            s_nmr = stat[:, 5:6]
            s_eps = stat[:, 6:7]
            P.emit("vector", lambda v: nc.vector.reduce_sum(s_sum, x_ap, AX.X),
                   reads=list(bufs_x), writes=[B_stat], tag="ln_sum", small=True)
            P.emit("scalar", lambda s: nc.scalar.activation(
                prod[:, :], x_ap, AF.Square, accum_out=s_sq),
                reads=list(bufs_x), writes=[B_prod, B_stat], tag="ln_sq", small=True)
            P.emit("vector", lambda v: nc.vector.tensor_scalar(
                s_mean, s_sum, 1.0 / D, None, ALU.mult),
                reads=[B_stat], writes=[B_stat], tag="ln_v1", small=True)
            P.emit("vector", lambda v: nc.vector.tensor_tensor(
                s_var, s_mean, s_mean, ALU.mult),
                reads=[B_stat], writes=[B_stat], tag="ln_v2", small=True)
            P.emit("vector", lambda v: nc.vector.scalar_tensor_tensor(
                s_var, s_sq, 1.0 / D, s_var, ALU.mult, ALU.subtract),
                reads=[B_stat], writes=[B_stat], tag="ln_v3", small=True)
            P.emit("scalar", lambda s: nc.scalar.activation(
                s_var, s_var, AF.Sqrt, bias=s_eps),
                reads=[B_stat], writes=[B_stat], tag="ln_sqrt", small=True)
            P.emit("vector", lambda v: nc.vector.reciprocal(s_rstd, s_var),
                   reads=[B_stat], writes=[B_stat], tag="ln_r1", small=True)
            P.emit("vector", lambda v: nc.vector.scalar_tensor_tensor(
                s_nmr, s_mean, -1.0, s_rstd, ALU.mult, ALU.mult),
                reads=[B_stat], writes=[B_stat], tag="ln_r2", small=True)
            P.emit("vector", lambda v: nc.vector.tensor_scalar(
                out_ap, x_ap, s_rstd, s_nmr, ALU.mult, ALU.add),
                reads=list(bufs_x) + [B_stat], writes=list(bufs_out), tag="ln_n")

        def tm2fm(src_ap, bufs_src, dst_ap, bufs_dst, tag):
            def tmm(tn):
                last = None
                for p2 in range(NPT):
                    bank = 6 + (p2 // 4)
                    last = nc.tensor.transpose(
                        psum[bank][:, (p2 % 4) * 128:(p2 % 4) * 128 + 128],
                        src_ap[:, p2 * 128:(p2 + 1) * 128], ident[:, :])
                return last
            P.emit("tensor", tmm, reads=list(bufs_src) + [B_const],
                   writes=[B_ps[6], B_ps[7]], tag=tag + "mm")
            P.emit("scalar", lambda s: nc.scalar.copy(dst_ap[:, 0:512], psum[6][:, :]),
                   reads=[B_ps[6]], writes=list(bufs_dst), tag=tag + "e0")
            P.emit("vector", lambda v: nc.vector.tensor_copy(
                dst_ap[:, 512:1024], psum[7][:, :]),
                reads=[B_ps[7]], writes=list(bufs_dst), tag=tag + "e1")

        def split_half(src_ap, bufs_src, half, tag):
            hi_ap = st_hi[:, half * D:(half + 1) * D]
            lo_ap = st_lo[:, half * D:(half + 1) * D]
            P.emit("scalar", lambda s, hi_ap=hi_ap: nc.scalar.copy(
                hi_ap, src_ap),
                reads=list(bufs_src), writes=[B_st[half]], tag=tag + "h")
            P.emit("vector", lambda v, hi_ap=hi_ap, lo_ap=lo_ap:
                   nc.vector.tensor_tensor(lo_ap, src_ap, hi_ap, ALU.subtract),
                   reads=list(bufs_src) + [B_st[half]], writes=[B_st[half]],
                   tag=tag + "l")

        wslot_state = {"n": 0}

        def next_wslot(use3=False):
            s = wslot_state["n"]
            wslot_state["n"] = (s + 1) % 2
            return s

        pbank_state = {"n": 0}

        def next_pbank():
            s = pbank_state["n"]
            pbank_state["n"] = (s + 1) % 4
            return s

        HKC = NPT // 2 * 2 * PW

        def load_wpiece(slot, name, pi_):
            for hh in range(2):
                P.emit("sync", lambda s, name=name, pi_=pi_, slot=slot, hh=hh:
                       s.dma_start(
                           out=wbuf[slot][:, hh * HKC:(hh + 1) * HKC],
                           in_=whl[name][pi_, :, hh * HKC:(hh + 1) * HKC]),
                       reads=[B_wsc], writes=[B_w[slot][hh]], tag="wload")

        def proj_piece(wslot, pbank, half, bias_ap, bias_buf=None):
            def mm(tn, pbank=pbank, bias_ap=bias_ap, wslot=wslot, half=half,
                   hh=0):
                last = None
                for dc in range(hh * 4, hh * 4 + 4):
                    sh = st_hi[:, half * D + dc * 128:half * D + (dc + 1) * 128]
                    sl = st_lo[:, half * D + dc * 128:half * D + (dc + 1) * 128]
                    whi_ap = wbuf[wslot][:, dc * 2 * PW:dc * 2 * PW + PW]
                    wlo_ap = wbuf[wslot][:, dc * 2 * PW + PW:(dc + 1) * 2 * PW]
                    last = nc.tensor.matmul(
                        psum[pbank][:, 0:PW], sh, whi_ap,
                        start=(dc == 0), stop=False)
                    last = nc.tensor.matmul(
                        psum[pbank][:, 0:PW], sh, wlo_ap,
                        start=False, stop=False)
                    last = nc.tensor.matmul(
                        psum[pbank][:, 0:PW], sl, whi_ap,
                        start=False,
                        stop=(hh == 1 and dc == NPT - 1 and bias_ap is None))
                if hh == 1 and bias_ap is not None:
                    last = nc.tensor.matmul(
                        psum[pbank][:, 0:PW], ones2[:, :], bias_ap,
                        start=False, stop=True)
                return last
            extra = [bias_buf] if bias_buf is not None else []
            def mmall(tn, mm=mm):
                mm(tn, hh=0)
                return mm(tn, hh=1)
            P.emit("tensor", mmall,
                   reads=[B_st[half], B_w[wslot][0], B_w[wslot][1],
                          B_const] + extra,
                   writes=[B_ps[pbank]], tag="proj")

        def stream(wname, dst_cb, bias_seg, tag, halves=(0, 1), use3=False):
            for pi_, c0 in enumerate(range(0, D, PW)):
                slot = next_wslot(use3)
                load_wpiece(slot, wname, pi_)
                bias_ap = None
                bias_buf = None
                if bias_seg is not None:
                    bsl = pi_ % 2
                    bias_ap = bias_stage[:, bsl * PW:(bsl + 1) * PW]
                    bias_buf = B_bias[bsl]
                    boff = BSEG[bias_seg] * D + c0
                    P.emit("gpsimd", lambda g, boff=boff, ba=bias_ap:
                           g.dma_start(out=ba, in_=bias_hl[:, boff:boff + PW]),
                           writes=[bias_buf], reads=[B_wsc], tag="bias",
                           dma=True)
                for half in halves:
                    pb_ = next_pbank()
                    proj_piece(slot, pb_, half, bias_ap, bias_buf=bias_buf)
                    dst_cb(c0, pb_, half)

        def emit_mix(kind, half, carry_ap, B_carry, dst_ap, bufs_dst):
            mu_sl = mu[:, kind * NPT:(kind + 1) * NPT]
            omu_sl = omu[:, kind * NPT:(kind + 1) * NPT]
            x3 = xlh(half).rearrange("p (a c) -> p a c", a=NPT)
            o3 = dst_ap.rearrange("p (a c) -> p a c", a=NPT)
            if half == 0:
                carry_src = [carry_ap[:, a2:a2 + 1] for a2 in range(NPT)]
                carry_reads = [B_carry]
            else:
                x3p = xlh(0).rearrange("p (a c) -> p a c", a=NPT)
                carry_src = [x3p[:, a2, 127:128] for a2 in range(NPT)]
                carry_reads = [B_xlnT[0]]

            def mix1(v):
                last = None
                for a2 in range(NPT):
                    last = nc.vector.tensor_scalar(
                        o3[:, a2, :], x3[:, a2, :], mu_sl[:, a2:a2 + 1], None,
                        ALU.mult)
                return last
            P.emit("vector", mix1, reads=[B_xlnT[half], B_const],
                   writes=list(bufs_dst), tag="mix1")

            def mix2(v):
                last = None
                for a2 in range(NPT):
                    last = nc.vector.scalar_tensor_tensor(
                        o3[:, a2, 1:128], x3[:, a2, 0:127], omu_sl[:, a2:a2 + 1],
                        o3[:, a2, 1:128], ALU.mult, ALU.add)
                    last = nc.vector.scalar_tensor_tensor(
                        o3[:, a2, 0:1], carry_src[a2], omu_sl[:, a2:a2 + 1],
                        o3[:, a2, 0:1], ALU.mult, ALU.add)
                return last
            P.emit("vector", mix2,
                   reads=[B_xlnT[half], B_const] + carry_reads,
                   writes=list(bufs_dst), tag="mix2")

        def ffn(ts, carry_ap, B_carry, tag):
            mixdst = (prod, q_fm)
            mixbuf = (B_prod, B_qfm)
            for i in range(2):
                ln_tile(xh(i), [B_x[i]], facc[:, :], [B_facc[0], B_facc[1]])
                tm2fm(facc[:, :], [B_facc[0], B_facc[1]], xlh(i), [B_xlnT[i]], f"{tag}lnT{i}")
            for i in range(2):
                emit_mix(0, i, carry_ap, B_carry, mixdst[i][:, :], [mixbuf[i]])
                split_half(mixdst[i][:, :], [mixbuf[i]], i, f"{tag}sxr{i}")

            def rsig_cb(c0, pb_, half):
                P.emit("scalar", lambda s, c0=c0, pb_=pb_, half=half:
                       nc.scalar.activation(
                           a_ch[:, half * D + c0:half * D + c0 + PW],
                           psum[pb_][:, 0:PW], AF.Sigmoid),
                       reads=[B_ps[pb_]], writes=[B_a[half]], tag=f"{tag}rsig")
            stream("r", rsig_cb, None, f"{tag}Wr", use3=True)
            for i in range(2):
                emit_mix(1, i, carry_ap, B_carry, mixdst[i][:, :], [mixbuf[i]])
                split_half(mixdst[i][:, :], [mixbuf[i]], i, f"{tag}sxk{i}")
            P.emit("vector", lambda v: nc.vector.tensor_copy(
                carry_ap[:, :],
                xlh(1).rearrange("p (a c) -> p a c", a=NPT)[:, :, 127]),
                reads=[B_xlnT[1]], writes=[B_carry], tag=f"{tag}carry", small=True)

            relu_t = (prod, facc)
            relu_b = ([B_prod], [B_facc[0], B_facc[1]])

            def krelu_cb(c0, pb_, half):
                P.emit("vector", lambda v, c0=c0, pb_=pb_, half=half:
                       nc.vector.tensor_scalar_max(
                           relu_t[half][:, c0:c0 + PW], psum[pb_][:, 0:PW], 0.0),
                       reads=[B_ps[pb_]], writes=relu_b[half],
                       tag=f"{tag}krelu")
            stream("k", krelu_cb, None, f"{tag}Wk", use3=True)

            for i in range(2):
                def ktr(tn, i=i):
                    last = None
                    for p2 in range(NPT):
                        bank = 6 + (p2 // 4)
                        last = nc.tensor.transpose(
                            psum[bank][:, (p2 % 4) * 128:(p2 % 4) * 128 + 128],
                            relu_t[i][:, p2 * 128:(p2 + 1) * 128], ident[:, :])
                    return last
                P.emit("tensor", ktr, reads=relu_b[i] + [B_const],
                       writes=[B_ps[6], B_ps[7]], tag=f"{tag}ktr{i}")
                P.emit("scalar", lambda s, i=i: nc.scalar.activation(
                    b_ch[:, i * D:i * D + 512], psum[6][:, :], AF.Square),
                    reads=[B_ps[6]], writes=[B_b[i]], tag=f"{tag}ksq0")
                P.emit("scalar", lambda s, i=i: nc.scalar.activation(
                    b_ch[:, i * D + 512:i * D + 1024], psum[7][:, :], AF.Square),
                    reads=[B_ps[7]], writes=[B_b[i]], tag=f"{tag}ksq1")
                split_half(bh(i), [B_b[i]], i, f"{tag}skt{i}")

            def v_cb(c0, pb_, half):
                vt = relu_t[half]
                vb = relu_b[half]
                P.emit("vector",
                       lambda v, c0=c0, pb_=pb_, half=half, vt=vt:
                       nc.vector.tensor_tensor(
                           vt[:, c0:c0 + PW],
                           a_ch[:, half * D + c0:half * D + c0 + PW],
                           psum[pb_][:, 0:PW], ALU.mult),
                       reads=[B_ps[pb_], B_a[half]], writes=vb, tag=f"{tag}vm")
                P.emit("vector",
                       lambda v, c0=c0, half=half, vt=vt:
                       nc.vector.scalar_tensor_tensor(
                           x_ch[:, half * D + c0:half * D + c0 + PW],
                           vt[:, c0:c0 + PW], 0.5,
                           x_ch[:, half * D + c0:half * D + c0 + PW],
                           ALU.mult, ALU.add),
                       reads=vb + [B_x[half]], writes=[B_x[half]], tag=f"{tag}va")
            stream("v", v_cb, None, f"{tag}Wv", use3=True)
            if ts[0] == 0 and tag[0] in "AC":
                dump(tag[0] + "_xmid", xh(0), D, [B_x[0]])

        def attn_core(t, half):
            qk3 = qk_cmp[:, :].rearrange("p (h o) -> p h o", h=H)
            for ci, m in enumerate(FAR_M):
                rk = NRANK + ci
                if t < m:
                    P.emit("vector", lambda v, rk=rk, qk3=qk3: nc.vector.memset(
                        qk3[:, :, rk:rk + 1], 0.0),
                        writes=[B_qkf], tag=f"fneg{t}_{ci}", small=True)
                    continue
                sl = (t - m) % RING
                P.emit("vector", lambda v, sl=sl, half=half: nc.vector.tensor_tensor(
                    prod[:, :], ah(half),
                    k_tm[:, sl * D:(sl + 1) * D], ALU.mult),
                    reads=[B_a[half], B_ktm[sl]], writes=[B_prod],
                    tag=f"fqk{t}_{ci}")
                P.emit("vector", lambda v, rk=rk, qk3=qk3: nc.vector.tensor_reduce(
                    qk3[:, :, rk:rk + 1],
                    prod[:, :].rearrange("p (h d) -> p h d", h=H), AX.X, ALU.add),
                    reads=[B_prod], writes=[B_qkf], tag=f"frd{t}_{ci}", small=True)
            tm2fm(ah(half), [B_a[half]], q_fm[:, :], [B_qfm], f"qT{t}")
            if t == 0:
                dump("B_qfm", q_fm[:, :], D, [B_qfm])
            ksl = t % RING
            kfsl = t % KFR
            tm2fm(k_tm[:, ksl * D:(ksl + 1) * D], [B_ktm[ksl]],
                  k_fm[:, kfsl * D:(kfsl + 1) * D], [B_kfm[kfsl]], f"kT{t}")
            cstart = {0: 192, 1: 64}.get(t, 0)
            for h in range(H):
                hp = h // 2
                base = 64 * (h % 2)
                bslot = h % 2
                bank = 2 + (h % 2)

                def bmm(tn, h=h, hp=hp, base=base, bank=bank, t=t):
                    last = None
                    segs = []
                    if t >= 2:
                        segs.append(((t - 2) % KFR, 64, 128, 0))
                    if t >= 1:
                        segs.append(((t - 1) % KFR, 0, 128, 64))
                    segs.append((t % KFR, 0, 128, 192))
                    for (sl, a0, a1, oc) in segs:
                        last = nc.tensor.matmul(
                            psum[bank][:, oc:oc + (a1 - a0)],
                            q_fm[base:base + 64, hp * 128:(hp + 1) * 128],
                            k_fm[base:base + 64,
                                 sl * D + hp * 128 + a0:sl * D + hp * 128 + a1],
                            start=True, stop=True)
                    last = nc.tensor.matmul(
                        psum[bank][:, WIN:WIN + NOFF],
                        q_fm[base:base + 64, hp * 128:(hp + 1) * 128],
                        sedup[base:base + 64, :], start=True, stop=True)
                    return last
                kf_reads = [B_kfm[(t - k) % KFR] for k in range(min(3, t + 1))]
                P.emit("tensor", bmm, reads=[B_qfm, B_const] + kf_reads,
                       writes=[B_ps[bank]], tag=f"band{t}_{h}")
                if t < 2:
                    P.emit("vector", lambda v, bslot=bslot: nc.vector.memset(
                        band_scr[:, bslot * WIN:bslot * WIN + WIN], 0.0),
                        writes=[B_band[bslot]], tag="bz")

                P.emit("scalar", lambda o, bank=bank, bslot=bslot,
                       cstart=cstart: nc.scalar.copy(
                           band_scr[:, bslot * WIN + cstart:bslot * WIN + WIN],
                           psum[bank][:, cstart:WIN]),
                       reads=[B_ps[bank]], writes=[B_band[bslot]],
                       tag=f"bev{t}_{h}")
                P.emit("scalar", lambda o, bank=bank, h=h:
                       nc.scalar.activation(
                           qs_cmp[:, h * NOFF:(h + 1) * NOFF],
                           psum[bank][:, WIN:WIN + NOFF], AF.Copy, bias=1.0),
                       reads=[B_ps[bank]], writes=[B_qs[h // 8]],
                       tag=f"qsev{t}_{h}")
                P.emit("gpsimd", lambda g, h=h, bslot=bslot: g.local_scatter(
                    qk_cmp[:, h * NOFF:h * NOFF + NRANK].bitcast(I16),
                    band_scr[:, bslot * WIN:bslot * WIN + WIN].bitcast(I16),
                    idxe[:, :], 128, 2 * NRANK, 2 * WIN),
                    reads=[B_band[bslot], B_const], writes=[B_qk[h // 8]],
                    tag=f"ext{t}_{h}")
            if t == 0:
                dump("B_qk", qk_cmp[:, :], H * NOFF, [B_qk[0], B_qk[1], B_qkf])
                dump("B_qs", qs_cmp[:, :], H * NOFF, [B_qs[0], B_qs[1]])
            pbT = (pbandT[:, :], q_fm[:, 0:PWIN])
            pbT_buf = (B_pbandT, B_qfm)

            def emit_pvmm(h):
                def pvmm(tn, h=h, t=t):
                    bank = 4 + (h // 8)
                    ccol = (h % 8) * 64
                    psl = h % 2
                    last = None
                    nch = 0
                    bank_first = (h == 0 or h == 8)
                    bank_last = (h == 7 or h == 15)
                    nvalid = sum(1 for c_ in range(3) if t - 2 + c_ >= 0)
                    for ch_ in range(3):
                        sl_t = t - 2 + ch_
                        if sl_t < 0:
                            continue
                        sl = sl_t % RING
                        last = nc.tensor.matmul(
                            psum[bank][:, ccol:ccol + 64],
                            pbT[psl][:, ch_ * 128:(ch_ + 1) * 128],
                            v_tm[:, sl * D + h * HD:sl * D + (h + 1) * HD],
                            start=(bank_first and nch == 0),
                            stop=(bank_last and nch == nvalid - 1),
                            skip_group_check=True)
                        nch += 1
                    return last
                vt_reads = [B_vtm[(t - 2 + c_) % RING] for c_ in range(3)
                            if t - 2 + c_ >= 0]
                P.emit("tensor", pvmm, reads=[pbT_buf[h % 2]] + vt_reads,
                       writes=[B_ps[4 + (h // 8)]], tag=f"pv{t}_{h}")

            pend = None
            for grp in range(2):
                hg0 = grp * 8
                cs = hg0 * NOFF
                ce = cs + 8 * NOFF
                pg = p_cmp[:, cs:ce]
                lgs = pg.rearrange("p (h o) -> p h o", h=8)
                P.emit("vector", lambda v, pg=pg, cs=cs, ce=ce:
                       nc.vector.tensor_tensor(
                           pg, qk_cmp[:, cs:ce], qs_cmp[:, cs:ce], ALU.mult),
                       reads=[B_qk[grp], B_qkf, B_qs[grp]], writes=[B_p[grp]],
                       tag=f"lg1_{grp}")
                P.emit("vector", lambda v, pg=pg, cs=cs, ce=ce:
                       nc.vector.scalar_tensor_tensor(
                           pg, pg, INV, pbc[:, cs:ce], ALU.mult, ALU.add),
                       reads=[B_p[grp], B_const], writes=[B_p[grp]],
                       tag=f"lg3_{grp}")
                for ci, m in enumerate(FAR_M):
                    if t < m:
                        rk = NRANK + ci
                        P.emit("vector", lambda v, rk=rk, lgs=lgs:
                               nc.vector.memset(lgs[:, :, rk:rk + 1], NEG),
                               writes=[B_p[grp]], tag=f"lgneg{t}_{ci}_{grp}",
                               small=True)
                if t == 0:
                    P.emit("vector", lambda v, lgs=lgs: nc.vector.tensor_tensor(
                        lgs[:, :, 0:NRANK], lgs[:, :, 0:NRANK],
                        mask0[:, :].unsqueeze(1).broadcast_to([128, 8, NRANK]),
                        ALU.add),
                        reads=[B_p[grp], B_const], writes=[B_p[grp]],
                        tag=f"lgmask_{grp}")
                elif t == 1:
                    P.emit("vector", lambda v, pg=pg: nc.vector.memset(
                        pg[0:64, :].rearrange(
                            "p (h o) -> p h o", h=8)[:, :, NRANK - 1:NRANK], NEG),
                        reads=[B_p[grp]], writes=[B_p[grp]],
                        tag=f"lgmask1_{grp}", small=True)
                if t == 0 and grp == 1:
                    dump("B_lg", p_cmp[:, :], H * NOFF, [B_p[0], B_p[1]])
                mx = red16[:, hg0:hg0 + 8]
                sm = red16[:, H + hg0:H + hg0 + 8]
                rc = rec16[:, hg0:hg0 + 8]
                P.emit("vector", lambda v, lgs=lgs, mx=mx: nc.vector.tensor_reduce(
                    mx.rearrange("p (h o) -> p h o", o=1), lgs, AX.X, ALU.max),
                    reads=[B_p[grp]], writes=[B_red[grp]], tag=f"smax{grp}",
                    small=True)
                P.emit("vector", lambda v, lgs=lgs, mx=mx: nc.vector.tensor_tensor(
                    lgs, lgs,
                    mx.rearrange("p (h o) -> p h o", o=1).broadcast_to(
                        [128, 8, NOFF]), ALU.subtract),
                    reads=[B_p[grp], B_red[grp]], writes=[B_p[grp]],
                    tag=f"ssub{grp}")
                P.emit("scalar", lambda s, pg=pg: nc.scalar.activation(
                    pg, pg, AF.Exp),
                    reads=[B_p[grp]], writes=[B_p[grp]], tag=f"sexp{grp}")
                P.emit("vector", lambda v, lgs=lgs, sm=sm: nc.vector.tensor_reduce(
                    sm.rearrange("p (h o) -> p h o", o=1), lgs, AX.X, ALU.add),
                    reads=[B_p[grp]], writes=[B_red[grp]], tag=f"ssum{grp}",
                    small=True)
                P.emit("vector", lambda v, rc=rc, sm=sm: nc.vector.reciprocal(
                    rc, sm),
                    reads=[B_red[grp]], writes=[B_rec[grp]], tag=f"srcp1{grp}",
                    small=True)
                P.emit("vector", lambda v, rc=rc, hg0=hg0: nc.vector.tensor_tensor(
                    rc, rc, gain16[:, hg0:hg0 + 8], ALU.mult),
                    reads=[B_rec[grp], B_const], writes=[B_rec[grp]],
                    tag=f"srcp2{grp}", small=True)
                P.emit("vector", lambda v, lgs=lgs, rc=rc: nc.vector.tensor_tensor(
                    lgs, lgs,
                    rc.rearrange("p (h o) -> p h o", o=1).broadcast_to(
                        [128, 8, NOFF]), ALU.mult),
                    reads=[B_p[grp], B_rec[grp]], writes=[B_p[grp]],
                    tag=f"smul{grp}")
                if t == 0 and grp == 1:
                    dump("B_p", p_cmp[:, :], H * NOFF, [B_p[0], B_p[1]])
                first = True
                facc_g = facc[:, grp * 512:(grp + 1) * 512]
                prod_g = prod[:, grp * 512:(grp + 1) * 512]
                for ci, m in enumerate(FAR_M):
                    rk = NRANK + ci
                    if t < m:
                        continue
                    sl = (t - m) % RING
                    pb_ap = lgs[:, :, rk:rk + 1].broadcast_to([128, 8, HD])
                    v_g = v_tm[:, sl * D + grp * 512:sl * D + (grp + 1) * 512]
                    if first:
                        P.emit("vector", lambda v, pb_ap=pb_ap, v_g=v_g,
                               facc_g=facc_g: nc.vector.tensor_tensor(
                                   facc_g.rearrange("p (h d) -> p h d", h=8),
                                   v_g.rearrange("p (h d) -> p h d", h=8),
                                   pb_ap, ALU.mult),
                               reads=[B_p[grp], B_vtm[sl]],
                               writes=[B_facc[grp]], tag=f"fpv{t}_{ci}_{grp}")
                        first = False
                    else:
                        P.emit("vector", lambda v, pb_ap=pb_ap, v_g=v_g,
                               prod_g=prod_g: nc.vector.tensor_tensor(
                                   prod_g.rearrange("p (h d) -> p h d", h=8),
                                   v_g.rearrange("p (h d) -> p h d", h=8),
                                   pb_ap, ALU.mult),
                               reads=[B_p[grp], B_vtm[sl]], writes=[B_prod],
                               tag=f"fpm{t}_{ci}_{grp}")
                        P.emit("vector", lambda v, facc_g=facc_g, prod_g=prod_g:
                               nc.vector.tensor_tensor(
                                   facc_g, facc_g, prod_g, ALU.add),
                               reads=[B_prod, B_facc[grp]],
                               writes=[B_facc[grp]], tag=f"fpa{t}_{ci}_{grp}")
                if first:
                    P.emit("vector", lambda v, facc_g=facc_g: nc.vector.memset(
                        facc_g, 0.0),
                        writes=[B_facc[grp]], tag=f"fz{grp}")
                for h in range(hg0, hg0 + 8):
                    psl = h % 2
                    tbank = 6 + psl
                    P.emit("gpsimd", lambda g, h=h, psl=psl: g.local_scatter(
                        pband[psl][:, :].bitcast(I16),
                        p_cmp[:, h * NOFF:h * NOFF + NRANK].bitcast(I16),
                        idxb[:, :], 128, 2 * PWIN, 2 * NRANK),
                        reads=[B_p[grp], B_const], writes=[B_pband[psl]],
                        tag=f"bld{t}_{h}")

                    def ptr(tn, psl=psl, tbank=tbank):
                        last = None
                        for ch_ in range(3):
                            last = nc.tensor.transpose(
                                psum[tbank][:, ch_ * 128:(ch_ + 1) * 128],
                                pband[psl][:, ch_ * 128:(ch_ + 1) * 128],
                                ident[:, :])
                        return last
                    P.emit("tensor", ptr, reads=[B_pband[psl], B_const],
                           writes=[B_ps[tbank]], tag=f"ptr{t}_{h}")
                    if pend is not None:
                        emit_pvmm(pend)
                    P.emit("scalar", lambda o, psl=psl, tbank=tbank:
                           nc.scalar.copy(
                               pbT[psl][:, :], psum[tbank][:, 0:PWIN]),
                           reads=[B_ps[tbank]], writes=[pbT_buf[psl]],
                           tag=f"pte{t}_{h}")
                    pend = h
            emit_pvmm(pend)
            for hf in range(2):
                P.emit("vector", lambda v, hf=hf: nc.vector.scalar_tensor_tensor(
                    prod[:, hf * 512:(hf + 1) * 512],
                    psum[4 + hf][:, :], 1.0,
                    facc[:, hf * 512:(hf + 1) * 512], ALU.mult, ALU.add),
                    reads=[B_ps[4 + hf], B_facc[hf]], writes=[B_prod],
                    tag=f"goadd{t}_{hf}")
            P.emit("vector", lambda v, half=half: nc.vector.tensor_tensor(
                prod[:, :], prod[:, :], bh(half), ALU.mult),
                reads=[B_prod, B_b[half]], writes=[B_prod], tag=f"gomul{t}")
            tm2fm(prod[:, :], [B_prod], xlh(half), [B_xlnT[half]], f"goT{t}")
            if t == 0:
                dump("B_go", xlh(0), D, [B_xlnT[0]])

        def attn(ts):
            for i in range(2):
                ln_tile(xh(i), [B_x[i]], facc[:, :], [B_facc[0], B_facc[1]])
                tm2fm(facc[:, :], [B_facc[0], B_facc[1]], xlh(i), [B_xlnT[i]], f"b{ts[i]}lnT")
                split_half(xlh(i), [B_xlnT[i]], i, f"b{ts[i]}sln")

            def qev_cb(c0, pb_, half):
                P.emit("scalar", lambda s, c0=c0, pb_=pb_, half=half:
                       nc.scalar.copy(
                           a_ch[:, half * D + c0:half * D + c0 + PW],
                           psum[pb_][:, 0:PW]),
                       reads=[B_ps[pb_]], writes=[B_a[half]], tag="qev")
            stream("q", qev_cb, "q", "Wq")

            def gev_cb(c0, pb_, half):
                P.emit("scalar", lambda s, c0=c0, pb_=pb_, half=half:
                       nc.scalar.activation(
                           b_ch[:, half * D + c0:half * D + c0 + PW],
                           psum[pb_][:, 0:PW], AF.Sigmoid),
                       reads=[B_ps[pb_]], writes=[B_b[half]], tag="gev")
            stream("g", gev_cb, "g", "Wg")

            for i, t in enumerate(ts):
                ksl = t % RING

                def kev_cb(c0, pb_, half, ksl=ksl):
                    P.emit("scalar", lambda s, c0=c0, pb_=pb_, ksl=ksl:
                           nc.scalar.copy(
                               k_tm[:, ksl * D + c0:ksl * D + c0 + PW],
                               psum[pb_][:, 0:PW]),
                           reads=[B_ps[pb_]], writes=[B_ktm[ksl]], tag="kev")
                stream("k2", kev_cb, "k2", f"Wk2_{t}", halves=(i,))

                def vev_cb(c0, pb_, half, ksl=ksl):
                    P.emit("scalar", lambda s, c0=c0, pb_=pb_, ksl=ksl:
                           nc.scalar.copy(
                               v_tm[:, ksl * D + c0:ksl * D + c0 + PW],
                               psum[pb_][:, 0:PW]),
                           reads=[B_ps[pb_]], writes=[B_vtm[ksl]], tag="vev")
                stream("v2", vev_cb, "v2", f"Wv2_{t}", halves=(i,))
                if t == 0:
                    dump("B_q", ah(0), D, [B_a[0]])
                    dump("B_k", k_tm[:, 0:D], D, [B_ktm[0]])
                    dump("B_v", v_tm[:, 0:D], D, [B_vtm[0]])
                    dump("B_gate", bh(0), D, [B_b[0]])
                attn_core(t, i)

            for i in range(2):
                split_half(xlh(i), [B_xlnT[i]], i, f"b{ts[i]}sgo")

            def oacc_cb(c0, pb_, half):
                x_sl = x_ch[:, half * D + c0:half * D + c0 + PW]
                P.emit("vector", lambda v, pb_=pb_, x_sl=x_sl, half=half:
                       nc.vector.tensor_tensor(
                           x_sl, x_sl, psum[pb_][:, 0:PW], ALU.add),
                       reads=[B_ps[pb_], B_x[half]], writes=[B_x[half]],
                       tag="oacc")
            stream("o", oacc_cb, "o", "Wo")
            if ts[0] == 0:
                dump("B_x", xh(0), D, [B_x[0]])

        assert n_tiles % 2 == 0
        for c in range(n_tiles // 2):
            ts = (2 * c, 2 * c + 1)
            for i, t in enumerate(ts):
                P.emit("sync", lambda s, t=t, i=i: s.dma_start(
                    out=xh(i), in_=x_in[t * 128:(t + 1) * 128, :]),
                    writes=[B_x[i]], tag=f"xload{t}")
            ffn(ts, carryA[:, :], B_carryA, f"A{c}_")
            attn(ts)
            ffn(ts, carryC[:, :], B_carryC, f"C{c}_")
            for i, t in enumerate(ts):
                P.emit("sync", lambda s, t=t, i=i: s.dma_start(
                    out=y_out[t * 128:(t + 1) * 128, :], in_=xh(i)),
                    reads=[B_x[i]], writes=[B_out[t]], tag=f"store{t}")

        P.emit("tensor", lambda tn: nc.tensor.nop(),
               reads=[B_out[t] for t in range(n_tiles)], tag="fence")

        runner = P.make_runner(sems)
        with nc.Block() as block:
            block.sync(lambda o: runner(o, "sync"))
            block.tensor(lambda o: runner(o, "tensor"))
            block.vector(lambda o: runner(o, "vector"))
            block.scalar(lambda o: runner(o, "scalar"))
            block.gpsimd(lambda o: runner(o, "gpsimd"))

    lower_extended_insts(nc)
    return nc


def _prep_inputs(inputs):
    consts = _host_consts()
    pos_bias = np.asarray(inputs["L0_attn_pos_bias"], np.float32)
    scale_embed = np.asarray(inputs["L0_attn_scale_embed"], np.float32)
    if_gain = np.asarray(inputs["L0_attn_if_gain"], np.float32)
    mu_r = np.asarray(inputs["L0_ffn_mu_r"], np.float32)
    mu_k = np.asarray(inputs["L0_ffn_mu_k"], np.float32)
    sedup = np.empty((128, NOFF), np.float32)
    sedup[0:64] = scale_embed.T
    sedup[64:128] = scale_embed.T
    pbconst = np.ascontiguousarray(
        np.broadcast_to(pos_bias.T.reshape(1, H * NOFF), (128, H * NOFF)))
    gain16 = np.ascontiguousarray(
        np.broadcast_to(if_gain.reshape(1, H), (128, H)))
    mu_in = np.empty((128, 2 * NPT), np.float32)
    mu_in[:, 0:NPT] = mu_r.reshape(NPT, 128).T
    mu_in[:, NPT:2 * NPT] = mu_k.reshape(NPT, 128).T
    common = dict(
        qkv_wT=np.ascontiguousarray(np.asarray(inputs["L0_attn_qkv_w"], np.float32).T),
        gate_wT=np.ascontiguousarray(np.asarray(inputs["L0_attn_gate_w"], np.float32).T),
        out_wT=np.ascontiguousarray(np.asarray(inputs["L0_attn_out_w"], np.float32).T),
        WrT=np.ascontiguousarray(np.asarray(inputs["L0_ffn_W_r"], np.float32).T),
        WkT=np.ascontiguousarray(np.asarray(inputs["L0_ffn_W_k"], np.float32).T),
        WvT=np.ascontiguousarray(np.asarray(inputs["L0_ffn_W_v"], np.float32).T),
        qkv_b=np.asarray(inputs["L0_attn_qkv_b"], np.float32).reshape(1, -1),
        gate_b=np.asarray(inputs["L0_attn_gate_b"], np.float32).reshape(1, -1),
        out_b=np.asarray(inputs["L0_attn_out_b"], np.float32).reshape(1, -1),
        sedup=sedup, pbconst=pbconst, gain16=gain16, mu_in=mu_in,
        identity=consts["identity"],
        idx_ext=consts["idx_ext"], idx_bld=consts["idx_bld"],
        mask0=consts["mask0"],
    )
    return common


def kernel(**inputs):
    x = np.ascontiguousarray(np.asarray(inputs["L0_attn_x"], np.float32))
    B = x.shape[0]
    common = _prep_inputs(inputs)
    nc = build_nc(n_tiles=32)
    in_maps = [dict(common, x=np.ascontiguousarray(x[b])) for b in range(B)]
    res = run_bass_kernel_spmd(nc, in_maps, core_ids=list(range(B)))
    return np.stack([res.results[b]["y"] for b in range(B)], axis=0)
